# version 10
# baseline (speedup 1.0000x reference)
import numpy as np
import ml_dtypes
from contextlib import ExitStack

import concourse.bass as bass
import concourse.mybir as mybir
from concourse.bass_utils import run_bass_kernel_spmd

F32 = mybir.dt.float32
BF16 = mybir.dt.bfloat16
AF = mybir.ActivationFunctionType
ALU = mybir.AluOpType
AX = mybir.AxisListType

NCORES = 8
NSEQ = 2
T = 2048
D = 2048
NH = 16
HD = 128
HID = 8192
NT = NSEQ * T // 128
NTS = T // 128
SCALE = 1.0 / float(np.sqrt(HD))
EPS = 1e-5
GT = 512
NG = NSEQ * T // GT


class Sem:
    def __init__(self, ctx, nc, name):
        self.h = ctx.enter_context(nc.semaphore(name))


def build_nc(debug=False):
    nc = bass.Bass("TRN2", target_bir_lowering=False)

    x_in = nc.dram_tensor("x_in", [NSEQ * T, D], F32, kind="ExternalInput")
    wqT = nc.dram_tensor("wqT", [16 * 2048, 128], F32, kind="ExternalInput")
    wkT = nc.dram_tensor("wkT", [16 * 2048, 128], F32, kind="ExternalInput")
    wvT = nc.dram_tensor("wvT", [4 * 2048, 512], F32, kind="ExternalInput")
    woT = nc.dram_tensor("woT", [4 * 2048, 512], F32, kind="ExternalInput")
    wgT = nc.dram_tensor("wgT", [64 * 2048, 128], F32, kind="ExternalInput")
    wuT = nc.dram_tensor("wuT", [64 * 2048, 128], F32, kind="ExternalInput")
    wdT = nc.dram_tensor("wdT", [8192, 2048], F32, kind="ExternalInput")
    masklT_in = nc.dram_tensor("masklT_in", [128, 128], BF16, kind="ExternalInput")
    ident_in = nc.dram_tensor("ident_in", [128, 128], BF16, kind="ExternalInput")

    skind = "ExternalOutput" if debug else "Internal"
    q_hbm = nc.dram_tensor("q_hbm", [128, NH * NSEQ * T], BF16, kind=skind)
    k_hbm = nc.dram_tensor("k_hbm", [128, NH * NSEQ * T], BF16, kind=skind)
    v_hbm = nc.dram_tensor("v_hbm", [NSEQ * NH * NTS * 128, 128], BF16, kind=skind)
    z_hbm = nc.dram_tensor("z_hbm", [128, NH * NSEQ * T], BF16, kind=skind)
    x2_hbm = nc.dram_tensor("x2_hbm", [NSEQ * T, D], F32, kind=skind)
    wg_bf = nc.dram_tensor("wg_bf", [64 * 128, 2048], BF16, kind=skind)
    wu_bf = nc.dram_tensor("wu_bf", [64 * 128, 2048], BF16, kind=skind)
    wd_bf = nc.dram_tensor("wd_bf", [8192, 2048], BF16, kind=skind)
    out = nc.dram_tensor("out", [NSEQ * T, D], F32, kind="ExternalOutput")
    if debug:
        dbg_ln2fm = nc.dram_tensor("dbg_ln2fm", [128, 16 * 512], BF16,
                                   kind="ExternalOutput")
        dbg_gated = nc.dram_tensor("dbg_gated", [128, 64 * 512], BF16,
                                   kind="ExternalOutput")

    with ExitStack() as top:
        sb = lambda n, s, d: top.enter_context(nc.sbuf_tensor(n, s, d))
        S = lambda n: Sem(top, nc, n).h

        sVself = S("sVself")
        vself_n = [0]

        def vhop(v, ins):
            vself_n[0] += 1
            ins.then_inc(sVself, 1)
            v.wait_ge(sVself, vself_n[0])

        masklT_sb = sb("masklT_sb", [128, 128], BF16)
        ident_sb = sb("ident_sb", [128, 128], BF16)

        phab = top.enter_context(ExitStack())
        ln_fm = phab.enter_context(nc.sbuf_tensor("ln_fm", [128, 16 * NT * 128], BF16))

        sA_x = [S(f"sA_x{p}") for p in range(4)]
        sA_var = S("sA_var"); sA_std = S("sA_std")
        sA_ln = S("sA_ln"); sA_tg = S("sA_tg"); sA_tev = S("sA_tev")
        sA_c = S("sA_c")

        with ExitStack() as ph:
            psb = lambda n, s, d: ph.enter_context(nc.sbuf_tensor(n, s, d))
            x_sb = [psb(f"xA{p}", [128, D], F32) for p in range(4)]
            ln_tm = [psb(f"lntmA{p}", [128, D], BF16) for p in range(4)]
            bnst = [psb(f"bnstA{p}", [128, 24], F32) for p in range(2)]
            stat = [psb(f"statA{p}", [128, 2], F32) for p in range(4)]
            veps = [psb(f"vepsA{p}", [128, 1], F32) for p in range(4)]
            stdv = [psb(f"stdA{p}", [128, 1], F32) for p in range(4)]
            rstd = [psb(f"rstdA{p}", [128, 1], F32) for p in range(4)]
            tr_ps = [ph.enter_context(nc.psum_tensor(f"trpsA{p}", [128, 512], F32))
                     for p in range(2)]

            with nc.Block() as blk:
                @blk.sync
                def _(sy):
                    sy.dma_start(masklT_sb[:, :], masklT_in[:, :]).then_inc(sA_c, 16)
                    sy.dma_start(ident_sb[:, :], ident_in[:, :]).then_inc(sA_c, 16)
                    for i in range(NT):
                        if i >= 4:
                            sy.wait_ge(sA_ln, i - 3)
                        sy.dma_start(x_sb[i % 4][:, :],
                                     x_in[i * 128:(i + 1) * 128, :])\
                            .then_inc(sA_x[i % 4], 16)

                @blk.vector
                def _(v):
                    for i in range(NT + 1):
                        if i < NT:
                            v.wait_ge(sA_x[i % 4], 16 * (i // 4 + 1))
                            for k in range(4):
                                ins = v.bn_stats(bnst[i % 2][:, 6 * k:6 * k + 6],
                                                 x_sb[i % 4][:, 512 * k:512 * k + 512])
                            vhop(v, ins)
                            vhop(v, v.bn_aggr(stat[i % 4][:, :], bnst[i % 2][:, :]))
                            v.tensor_scalar_add(veps[i % 4][:, 0:1],
                                                stat[i % 4][:, 1:2], EPS)\
                                .then_inc(sA_var, 1)
                        if i >= 1:
                            j = i - 1
                            v.wait_ge(sA_std, j + 1)
                            vhop(v, v.reciprocal(rstd[j % 4][:, 0:1],
                                                 stdv[j % 4][:, 0:1]))
                            if j >= 4:
                                v.wait_ge(sA_tg, 4 * (j - 3))
                            v.tensor_scalar(ln_tm[j % 4][:, :], x_sb[j % 4][:, :],
                                            stat[j % 4][:, 0:1], rstd[j % 4][:, 0:1],
                                            ALU.subtract, ALU.mult)\
                                .then_inc(sA_ln, 1)

                @blk.scalar
                def _(s):
                    for i in range(NT + 1):
                        if i < NT:
                            s.wait_ge(sA_var, i + 1)
                            if i >= 4:
                                s.wait_ge(sA_ln, i - 3)
                            s.activation(stdv[i % 4][:, 0:1], veps[i % 4][:, 0:1],
                                         AF.Sqrt).then_inc(sA_std, 1)
                        if i >= 1:
                            j = i - 1
                            for g in range(4):
                                gi = 4 * j + g
                                s.wait_ge(sA_tg, gi + 1)
                                for c in range(4):
                                    kc = 4 * g + c
                                    ins = s.activation(
                                        ln_fm[:, kc * (NT * 128) + j * 128:
                                              kc * (NT * 128) + (j + 1) * 128],
                                        tr_ps[gi % 2][:, c * 128:(c + 1) * 128],
                                        AF.Copy)
                                    if c == 3:
                                        ins.then_inc(sA_tev, 1)

                @blk.tensor
                def _(t):
                    t.wait_ge(sA_c, 32)
                    for j in range(NT):
                        t.wait_ge(sA_ln, j + 1)
                        for g in range(4):
                            gi = 4 * j + g
                            if gi >= 2:
                                t.wait_ge(sA_tev, gi - 1)
                            for c in range(4):
                                kc = 4 * g + c
                                ins = t.matmul(tr_ps[gi % 2][:, c * 128:(c + 1) * 128],
                                               ln_tm[j % 4][:, kc * 128:(kc + 1) * 128],
                                               ident_sb[:, :], start=True, stop=True)
                                if c == 3:
                                    ins.then_inc(sA_tg, 1)

                @blk.gpsimd
                def _(g):
                    g.wait_ge(sA_tev, 4 * NT)

        sB_wd = [S("sB_wd0"), S("sB_wd1")]
        sB_od = [S("sB_od0"), S("sB_od1")]
        sB_wc = S("sB_wc"); sB_mm = S("sB_mm")
        sB_ev = S("sB_ev"); sB_pe = S("sB_pe")

        with ExitStack() as ph:
            psb = lambda n, s, d: ph.enter_context(nc.sbuf_tensor(n, s, d))
            wb = [psb(f"wbB{p}", [128, 2048], BF16) for p in range(2)]
            ev = [psb(f"evB{p}", [128, 512], BF16) for p in range(2)]
            qkps = [ph.enter_context(nc.psum_tensor(f"qkpsB{p}", [128, 512], F32))
                    for p in range(2)]

            with nc.Block() as blk:
                @blk.gpsimd
                def _(g):
                    for it in range(32):
                        w, oc = divmod(it, 16)
                        src = (wqT if w == 0 else wkT)
                        if it >= 2:
                            g.wait_ge(sB_mm, 8 * (it - 1))
                        g.dma_start(
                            wb[it % 2][:, :].rearrange("p (kc h) -> p kc h", kc=16),
                            src[oc * 2048:(oc + 1) * 2048, :]
                            .rearrange("(kc p) h -> p kc h", p=128))\
                            .then_inc(sB_wd[it % 2], 16)

                @blk.tensor
                def _(t):
                    for it in range(32):
                        t.wait_ge(sB_wd[it % 2], 16 * (it // 2 + 1))
                        for gg in range(8):
                            j = it * 8 + gg
                            if j >= 2:
                                t.wait_ge(sB_ev, j - 1)
                            for kc in range(16):
                                ins = t.matmul(
                                    qkps[j % 2][:, :],
                                    wb[it % 2][:, kc * 128:(kc + 1) * 128],
                                    ln_fm[:, kc * (NT * 128) + gg * 512:
                                          kc * (NT * 128) + (gg + 1) * 512],
                                    start=(kc == 0), stop=(kc == 15))
                                if kc == 15:
                                    ins.then_inc(sB_mm, 1)

                @blk.vector
                def _(v):
                    for j in range(256):
                        mult = SCALE if j < 128 else 1.0
                        v.wait_ge(sB_mm, j + 1)
                        if j >= 2:
                            v.wait_ge(sB_od[j % 2], 16 * (j // 2))
                        v.tensor_scalar_mul(ev[j % 2][:, :],
                                            qkps[j % 2][:, :], mult)\
                            .then_inc(sB_ev, 1)

                @blk.scalar
                def _(s):
                    for j in range(256):
                        it, gg = divmod(j, 8)
                        w, oc = divmod(it, 16)
                        dst = (q_hbm if w == 0 else k_hbm)
                        s.wait_ge(sB_ev, j + 1)
                        s.dma_start(dst[:, oc * (NT * 128) + gg * 512:
                                        oc * (NT * 128) + (gg + 1) * 512],
                                    ev[j % 2][:, :]).then_inc(sB_od[j % 2], 16)
                    s.wait_ge(sB_od[0], 16 * 128)
                    s.wait_ge(sB_od[1], 16 * 128)

        sV_od = [S("sV_od0"), S("sV_od1")]
        sV_wd = [S("sV_wd0"), S("sV_wd1")]
        sV_wc = S("sV_wc"); sV_mm = S("sV_mm")
        sV_ev = S("sV_ev"); sV_pe = S("sV_pe")

        with ExitStack() as ph:
            psb = lambda n, s, d: ph.enter_context(nc.sbuf_tensor(n, s, d))
            vwb = [psb(f"vwbB{p}", [128, 8192], BF16) for p in range(2)]
            vev = [psb(f"vevB{p}", [128, 512], BF16) for p in range(2)]
            vps = [ph.enter_context(nc.psum_tensor(f"vpsB{p}", [128, 512], F32))
                   for p in range(2)]

            with nc.Block() as blk:
                @blk.gpsimd
                def _(g):
                    for ob in range(4):
                        if ob >= 2:
                            g.wait_ge(sV_mm, NT * (ob - 1))
                        g.dma_start(
                            vwb[ob % 2][:, :].rearrange("p (kc o) -> p kc o", kc=16),
                            wvT[ob * 2048:(ob + 1) * 2048, :]
                            .rearrange("(kc p) o -> p kc o", p=128))\
                            .then_inc(sV_wd[ob % 2], 16)

                @blk.tensor
                def _(t):
                    for ob in range(4):
                        t.wait_ge(sV_wd[ob % 2], 16 * (ob // 2 + 1))
                        for i in range(NT):
                            j = ob * NT + i
                            if j >= 2:
                                t.wait_ge(sV_ev, j - 1)
                            for kc in range(16):
                                ins = t.matmul(
                                    vps[j % 2][:, :],
                                    ln_fm[:, kc * (NT * 128) + i * 128:
                                          kc * (NT * 128) + (i + 1) * 128],
                                    vwb[ob % 2][:, kc * 512:(kc + 1) * 512],
                                    start=(kc == 0), stop=(kc == 15))
                                if kc == 15:
                                    ins.then_inc(sV_mm, 1)

                @blk.vector
                def _(v):
                    for j in range(4 * NT):
                        v.wait_ge(sV_mm, j + 1)
                        if j >= 2:
                            v.wait_ge(sV_od[j % 2], 64 * (j // 2))
                        v.tensor_copy(vev[j % 2][:, :], vps[j % 2][:, :])\
                            .then_inc(sV_ev, 1)

                @blk.scalar
                def _(s):
                    for j in range(4 * NT):
                        ob, i = divmod(j, NT)
                        sq, jt = divmod(i, NTS)
                        s.wait_ge(sV_ev, j + 1)
                        for hh in range(4):
                            h = ob * 4 + hh
                            r0 = ((sq * NH + h) * NTS + jt) * 128
                            s.dma_start(v_hbm[r0:r0 + 128, :],
                                        vev[j % 2][:, hh * 128:(hh + 1) * 128])\
                                .then_inc(sV_od[j % 2], 16)
                    s.wait_ge(sV_od[0], 64 * 64)
                    s.wait_ge(sV_od[1], 64 * 64)

        phab.close()

        sC_qkv = [S("sC_qkv0"), S("sC_qkv1")]
        sC_s = S("sC_s")
        sC_sevA = S("sC_sevA"); sC_sevB = S("sC_sevB")
        sC_negmx = S("sC_negmx"); sC_exp = S("sC_exp"); sC_pnorm = S("sC_pnorm")
        sC_tg = S("sC_tg"); sC_evA = S("sC_evA")
        sC_z = S("sC_z"); sC_zev = S("sC_zev")
        sC_zd = [S("sC_zd0"), S("sC_zd1")]
        sC_hdone = S("sC_hdone")
        sP_in = [S("sP_in0"), S("sP_in1")]
        sP_out = [S("sP_out0"), S("sP_out1")]

        def cntB(r):
            hh_full, i_part = divmod(r, 16)
            return hh_full * 8 + max(0, i_part - 8)

        grp_tab = []
        row_first_g = []
        absg = 0
        for hh in range(32):
            for i in range(16):
                row = []
                row_first_g.append(absg)
                for js in range(0, i + 1, 4):
                    row.append((absg, js, min(4, i + 1 - js)))
                    absg += 1
                grp_tab.append(row)
        NGRP = absg

        with ExitStack() as ph:
            psb = lambda n, s, d: ph.enter_context(nc.sbuf_tensor(n, s, d))
            q_sb = [psb(f"qC{p}", [128, 2048], BF16) for p in range(2)]
            k_sb = [psb(f"kC{p}", [128, 2048], BF16) for p in range(2)]
            v_sb = [psb(f"vC{p}", [128, 2048], BF16) for p in range(2)]
            scores = [psb(f"scC{p}", [128, 2048], F32) for p in range(2)]
            p_sb = [psb(f"pC{p}", [128, 2048], BF16) for p in range(2)]
            pn_sb = [psb(f"pnC{p}", [128, 2048], BF16) for p in range(4)]
            pt_sb = psb("ptC", [128, 16 * 512], BF16)
            zh_sb = [psb(f"zhC{p}", [128, 2048], BF16) for p in range(2)]
            mx = [psb(f"mxC{p}", [128, 1], F32) for p in range(2)]
            mxb = [psb(f"mxbC{p}", [128, 1], F32) for p in range(2)]
            negmx = [psb(f"ngC{p}", [128, 1], F32) for p in range(2)]
            sm = [psb(f"smC{p}", [128, 1], F32) for p in range(2)]
            rc = [psb(f"rcC{p}", [128, 1], F32) for p in range(2)]
            s_ps = ph.enter_context(nc.psum_tensor("spsC", [128, 2048], F32))
            pt_ps = [ph.enter_context(nc.psum_tensor(f"ptpsC{p}", [128, 512], F32))
                     for p in range(2)]
            z_ps = [ph.enter_context(nc.psum_tensor(f"zpsC{p}", [128, 512], F32))
                    for p in range(2)]

            ptv = pt_sb[:, :].rearrange("p (j c) -> p j c", j=16)
            ptsl = pt_sb[:, :].rearrange("p (j s c) -> p j s c", j=16, s=4)

            def qkv_load(sy, hh):
                sq, h = divmod(hh, 16)
                par = hh % 2
                base = h * (NT * 128) + sq * T
                sy.dma_start(q_sb[par][:, :], q_hbm[:, base:base + T])\
                    .then_inc(sC_qkv[par], 16)
                sy.dma_start(k_sb[par][:, :], k_hbm[:, base:base + T])\
                    .then_inc(sC_qkv[par], 16)
                r0 = (sq * NH + h) * NTS * 128
                sy.dma_start(
                    v_sb[par][:, :].rearrange("p (j hd) -> p j hd", j=16),
                    v_hbm[r0:r0 + NTS * 128, :]
                    .rearrange("(j p) hd -> p j hd", p=128))\
                    .then_inc(sC_qkv[par], 16)

            with nc.Block() as blk:
                @blk.sync
                def _(sy):
                    qkv_load(sy, 0)
                    qkv_load(sy, 1)
                    for hh in range(1, 31):
                        sy.wait_ge(sC_z, 4 * hh)
                        qkv_load(sy, hh + 1)
                    sy.wait_ge(sC_zd[0], 16 * 16)
                    sy.wait_ge(sC_zd[1], 16 * 16)

                @blk.tensor
                def _(t):
                    for hh in range(32):
                        par = hh % 2
                        for k in range(19):
                            if k <= 15:
                                i = k; r = hh * 16 + i
                                if k == 0:
                                    t.wait_ge(sC_qkv[hh % 2], 48 * (hh // 2 + 1))
                                t.wait_ge(sC_sevA, r)
                                if i >= 8:
                                    t.wait_ge(sC_sevB, cntB(r))
                                tot = (i + 1) * 128
                                nch = (tot + 511) // 512
                                dch = i // 4
                                for c in range(nch):
                                    wc = min(512, tot - c * 512)
                                    t.matmul(
                                        s_ps[:, c * 512:c * 512 + wc],
                                        q_sb[par][:, i * 128:(i + 1) * 128],
                                        k_sb[par][:, c * 512:c * 512 + wc],
                                        start=True, stop=(c != dch))
                                t.matmul(s_ps[:, i * 128:(i + 1) * 128],
                                         masklT_sb[:, :], ident_sb[:, :],
                                         start=False, stop=True)\
                                    .then_inc(sC_s, 1)
                            if k >= 3:
                                i2 = k - 3; r2 = hh * 16 + i2
                                blk = i2 // 4
                                t.wait_ge(sC_pnorm, r2 + 1)
                                for (g, js, jc) in grp_tab[r2]:
                                    if g >= 2:
                                        t.wait_ge(sC_evA, g - 1)
                                    for u in range(jc):
                                        jj = js + u
                                        ins = t.matmul(
                                            pt_ps[g % 2][:, u * 128:(u + 1) * 128],
                                            pn_sb[i2 % 4][:, jj * 128:(jj + 1) * 128],
                                            ident_sb[:, :], start=True, stop=True)
                                        if u == jc - 1:
                                            ins.then_inc(sC_tg, 1)
                                if i2 % 4 == 3:
                                    zb = hh * 4 + blk
                                    if zb >= 2:
                                        t.wait_ge(sC_zev, zb - 1)
                                    g_last = row_first_g[r2] + len(grp_tab[r2]) - 1
                                    t.wait_ge(sC_evA, g_last + 1)
                                    nj = 4 * blk + 4
                                    for jj in range(nj):
                                        lo = max(0, jj - 4 * blk)
                                        ins = t.matmul(
                                            z_ps[blk % 2][:, lo * 128:512],
                                            v_sb[par][:, jj * 128:(jj + 1) * 128],
                                            ptv[:, jj, lo * 128:512],
                                            start=(jj == 0), stop=(jj == nj - 1))
                                        if jj == nj - 1:
                                            ins.then_inc(sC_z, 1)


                @blk.vector
                def _(v):
                    for hh in range(32):
                        for k in range(19):
                            if k <= 15:
                                i = k; r = hh * 16 + i
                                v.wait_ge(sC_s, r + 1)
                                if r >= 2:
                                    v.wait_ge(sC_exp, r - 1)
                                tot = (i + 1) * 128
                                wA = min(tot, 1024)
                                v.tensor_scalar(scores[i % 2][:, 0:wA],
                                                s_ps[:, 0:wA], 0.0, None,
                                                ALU.add, ALU.max,
                                                accum_out=mx[i % 2][:, 0:1])\
                                    .then_inc(sC_sevA, 1)
                                v.wait_ge(sC_sevA, r + 1)
                                if tot > 1024:
                                    v.tensor_scalar(scores[i % 2][:, 1024:tot],
                                                    s_ps[:, 1024:tot], 0.0, None,
                                                    ALU.add, ALU.max,
                                                    accum_out=mxb[i % 2][:, 0:1])\
                                        .then_inc(sC_sevB, 1)
                                    v.wait_ge(sC_sevB, cntB(r) + 1)
                                    v.tensor_max(mx[i % 2][:, 0:1],
                                                 mx[i % 2][:, 0:1],
                                                 mxb[i % 2][:, 0:1])
                                v.tensor_scalar_mul(negmx[i % 2][:, 0:1],
                                                    mx[i % 2][:, 0:1], -1.0)\
                                    .then_inc(sC_negmx, 1)
                            if 1 <= k <= 16:
                                i1 = k - 1; r1 = hh * 16 + i1
                                v.wait_ge(sC_exp, r1 + 1)
                                vhop(v, v.reciprocal(rc[i1 % 2][:, 0:1],
                                                     sm[i1 % 2][:, 0:1]))
                                if r1 >= 4:
                                    g_end = row_first_g[r1 - 4] + len(grp_tab[r1 - 4])
                                    v.wait_ge(sC_tg, g_end)
                                tot = (i1 + 1) * 128
                                v.tensor_scalar(pn_sb[i1 % 4][:, 0:tot],
                                                p_sb[i1 % 2][:, 0:tot],
                                                rc[i1 % 2][:, 0:1], None, ALU.mult)\
                                    .then_inc(sC_pnorm, 1)

                @blk.scalar
                def _(s):
                    for hh in range(32):
                        for k in range(19):
                            if 1 <= k <= 16:
                                i1 = k - 1; r1 = hh * 16 + i1
                                s.wait_ge(sC_negmx, r1 + 1)
                                if r1 >= 2:
                                    s.wait_ge(sC_pnorm, r1 - 1)
                                tot = (i1 + 1) * 128
                                s.activation(p_sb[i1 % 2][:, 0:tot],
                                             scores[i1 % 2][:, 0:tot], AF.Exp,
                                             bias=negmx[i1 % 2][:, 0:1],
                                             accum_out=sm[i1 % 2][:, 0:1])\
                                    .then_inc(sC_exp, 1)
                            if k >= 3:
                                i2 = k - 3; r2 = hh * 16 + i2
                                blk = i2 // 4
                                srow = i2 % 4
                                first = True
                                for (g, js, jc) in grp_tab[r2]:
                                    s.wait_ge(sC_tg, g + 1)
                                    if first:
                                        s.wait_ge(sC_z, hh * 4 + blk)
                                        first = False
                                    s.activation(
                                        ptsl[:, js:js + jc, srow, :],
                                        pt_ps[g % 2][:, 0:jc * 128]
                                        .rearrange("p (j c) -> p j c", j=jc),
                                        AF.Copy).then_inc(sC_evA, 1)
                                if srow == 3:
                                    zb = hh * 4 + blk
                                    s.wait_ge(sC_z, zb + 1)
                                    if blk == 0 and hh >= 2:
                                        s.wait_ge(sC_zd[hh % 2], 16 * (hh // 2))
                                    s.activation(
                                        zh_sb[hh % 2][:, blk * 512:(blk + 1) * 512],
                                        z_ps[blk % 2][:, :], AF.Copy)\
                                        .then_inc(sC_zev, 1)
                        sq, h = divmod(hh, 16)
                        s.wait_ge(sC_zev, 4 * (hh + 1))
                        s.dma_start(z_hbm[:, h * (NT * 128) + sq * T:
                                          h * (NT * 128) + sq * T + T],
                                    zh_sb[hh % 2][:, :]).then_inc(sC_zd[hh % 2], 16)
                    s.wait_ge(sC_zd[0], 16 * 16)
                    s.wait_ge(sC_zd[1], 16 * 16)

                @blk.gpsimd
                def _(g):
                    np_ = 0
                    for b in range(64):
                        g.dma_start(wg_bf[b * 128:(b + 1) * 128, :]
                                    .rearrange("p (kc h) -> p kc h", kc=16),
                                    wgT[b * 2048:(b + 1) * 2048, :]
                                    .rearrange("(kc p) h -> p kc h", p=128))\
                            .then_inc(sP_out[0], 16)
                        g.dma_start(wu_bf[b * 128:(b + 1) * 128, :]
                                    .rearrange("p (kc h) -> p kc h", kc=16),
                                    wuT[b * 2048:(b + 1) * 2048, :]
                                    .rearrange("(kc p) h -> p kc h", p=128))\
                            .then_inc(sP_out[0], 16)
                        g.dma_start(wd_bf[b * 128:(b + 1) * 128, :],
                                    wdT[b * 128:(b + 1) * 128, :])\
                            .then_inc(sP_out[0], 16)
                        np_ += 3
                    g.wait_ge(sP_out[0], 16 * np_)

        sW_zd = [S("sW_zd0"), S("sW_zd1"), S("sW_zd2")]
        sW_xd = [S("sW_xd0"), S("sW_xd1")]
        sW_x2d = [S("sW_x2d0"), S("sW_x2d1")]
        sW_wd = S("sW_wd"); sW_wc = S("sW_wc")
        sW_mm = S("sW_mm"); sW_mmg = S("sW_mmg")
        sW_ev = S("sW_ev"); sW_pe = S("sW_pe")

        with ExitStack() as ph:
            psb = lambda n, s, d: ph.enter_context(nc.sbuf_tensor(n, s, d))
            c2wb = [psb(f"c2wb{p}", [128, 8192], BF16) for p in range(2)]
            zg_sb = [psb(f"zgC2{p}", [128, 8192], BF16) for p in range(3)]
            xo_sb = [psb(f"xoC2{p}", [128, 512], F32) for p in range(2)]
            x2o = [psb(f"x2oC2{p}", [128, 512], F32) for p in range(2)]
            c2ps = [ph.enter_context(nc.psum_tensor(f"c2ps{p}", [128, 512], F32))
                    for p in range(2)]

            with nc.Block() as blk:
                @blk.gpsimd
                def _(g):
                    for ob in range(4):
                        if ob >= 2:
                            g.wait_ge(sW_mm, 32 * (ob - 1))
                        g.dma_start(
                            c2wb[ob % 2][:, :].rearrange("p (dc o) -> p dc o", dc=16),
                            woT[ob * 2048:(ob + 1) * 2048, :]
                            .rearrange("(dc p) o -> p dc o", p=128))\
                            .then_inc(sW_wd, 16)

                @blk.sync
                def _(sy):
                    for ob in range(4):
                        for grp in range(NG):
                            g_it = ob * NG + grp
                            if g_it >= 3:
                                sy.wait_ge(sW_mm, 4 * (g_it - 2))
                            sy.dma_start(
                                zg_sb[g_it % 3][:, :]
                                .rearrange("p (dc c) -> p dc c", dc=16),
                                z_hbm[:, :]
                                .rearrange("p (dc t) -> p dc t", dc=16)
                                [:, :, grp * 512:(grp + 1) * 512])\
                                .then_inc(sW_zd[g_it % 3], 16)
                            for tc in range(4):
                                it = g_it * 4 + tc
                                tg = grp * 4 + tc
                                if it >= 2:
                                    sy.wait_ge(sW_ev, it - 1)
                                sy.dma_start(
                                    xo_sb[it % 2][:, :],
                                    x_in[tg * 128:(tg + 1) * 128,
                                         ob * 512:(ob + 1) * 512])\
                                    .then_inc(sW_xd[it % 2], 16)

                @blk.tensor
                def _(t):
                    for ob in range(4):
                        t.wait_ge(sW_wd, 16 * (ob + 1))
                        for grp in range(NG):
                            g_it = ob * NG + grp
                            t.wait_ge(sW_zd[g_it % 3], 16 * (g_it // 3 + 1))
                            for tc in range(4):
                                it = g_it * 4 + tc
                                if it >= 2:
                                    t.wait_ge(sW_ev, it - 1)
                                for dc in range(16):
                                    ins = t.matmul(
                                        c2ps[it % 2][:, :],
                                        zg_sb[g_it % 3][:, dc * 512 + tc * 128:
                                                        dc * 512 + (tc + 1) * 128],
                                        c2wb[ob % 2][:, dc * 512:(dc + 1) * 512],
                                        start=(dc == 0), stop=(dc == 15))
                                    if dc == 15:
                                        ins.then_inc(sW_mm, 1)

                @blk.vector
                def _(v):
                    for it in range(128):
                        v.wait_ge(sW_mm, it + 1)
                        v.wait_ge(sW_xd[it % 2], 16 * (it // 2 + 1))
                        if it >= 2:
                            v.wait_ge(sW_x2d[it % 2], 16 * (it // 2))
                        v.tensor_tensor(x2o[it % 2][:, :], c2ps[it % 2][:, :],
                                        xo_sb[it % 2][:, :], ALU.add)\
                            .then_inc(sW_ev, 1)

                @blk.scalar
                def _(s):
                    for it in range(128):
                        g_it, tc = divmod(it, 4)
                        ob, grp = divmod(g_it, NG)
                        tg = grp * 4 + tc
                        s.wait_ge(sW_ev, it + 1)
                        s.dma_start(x2_hbm[tg * 128:(tg + 1) * 128,
                                           ob * 512:(ob + 1) * 512],
                                    x2o[it % 2][:, :]).then_inc(sW_x2d[it % 2], 16)
                    s.wait_ge(sW_x2d[0], 16 * 64)
                    s.wait_ge(sW_x2d[1], 16 * 64)

        sD_x2 = [S(f"sD_x2{p}") for p in range(4)]
        sD_wgd = [S("sD_wgd0"), S("sD_wgd1")]
        sD_dwd = [S(f"sD_dwd{p}") for p in range(4)]
        sD_od = [S("sD_od0"), S("sD_od1")]
        sD_var = S("sD_var"); sD_std = S("sD_std")
        sD_ln = S("sD_ln"); sD_tg = S("sD_tg"); sD_tev = S("sD_tev")
        sD_gu = S("sD_gu")
        sD_g = S("sD_g"); sD_u = S("sD_u"); sD_si = S("sD_si"); sD_mu = S("sD_mu")
        sD_dcons = S("sD_dcons"); sD_d = S("sD_d")
        sD_ev = S("sD_ev"); sD_dbg = S("sD_dbg")

        sD_xo = [S("sD_xo0"), S("sD_xo1")]

        with ExitStack() as ph:
            psb = lambda n, s, d: ph.enter_context(nc.sbuf_tensor(n, s, d))
            x2t = [psb(f"x2tD{p}", [128, D], F32) for p in range(4)]
            xo_d = [psb(f"xoD{p}", [128, 512], F32) for p in range(2)]
            ln2fm = psb("ln2fmD", [128, 16 * 512], BF16)
            gated = psb("gatedD", [128, 64 * 512], BF16)
            wgb = [psb(f"wgbD{p}", [128, 2048], BF16) for p in range(2)]
            wub = [psb(f"wubD{p}", [128, 2048], BF16) for p in range(2)]
            wdb = [psb(f"wdbD{p}", [128, 2048], BF16) for p in range(4)]
            silu_sb = [psb(f"siluD{p}", [128, 512], F32) for p in range(2)]
            outo = [psb(f"outoD{p}", [128, 512], F32) for p in range(2)]
            ln2tm = [psb(f"ln2tmD{p}", [128, D], BF16) for p in range(4)]
            bnst2 = [psb(f"bnstD{p}", [128, 24], F32) for p in range(2)]
            stat2 = [psb(f"statD{p}", [128, 2], F32) for p in range(4)]
            veps2 = [psb(f"vepsD{p}", [128, 1], F32) for p in range(4)]
            stdv2 = [psb(f"stdD{p}", [128, 1], F32) for p in range(4)]
            rstd2 = [psb(f"rstdD{p}", [128, 1], F32) for p in range(4)]
            g_ps = ph.enter_context(nc.psum_tensor("gpsD", [128, 512], F32))
            u_ps = ph.enter_context(nc.psum_tensor("upsD", [128, 512], F32))
            d_ps = [ph.enter_context(nc.psum_tensor(f"dpsD{p}", [128, 512], F32))
                    for p in range(4)]
            tr2_ps = [ph.enter_context(nc.psum_tensor(f"tr2D{p}", [128, 512], F32))
                      for p in range(2)]

            with nc.Block() as blk:
                @blk.sync
                def _(sy):
                    def x2t_loads(gx, rng=range(4)):
                        for tt in rng:
                            ti = gx * 4 + tt
                            if ti >= 4:
                                sy.wait_ge(sD_ln, ti - 3)
                            sy.dma_start(
                                x2t[ti % 4][:, :],
                                x2_hbm[ti * 128:(ti + 1) * 128, :])\
                                .then_inc(sD_x2[ti % 4], 16)

                    for g in range(NG):
                        if g == 0:
                            x2t_loads(0)
                            x2t_loads(1, range(3))
                        elif g + 1 < NG:
                            x2t_loads(g + 1, range(3))
                        def dcons_cnt(upto):
                            return upto - upto // 16

                        for ob in range(4):
                            for m in range(16):
                                st = (g * 4 + ob) * 16 + m
                                if st >= 4:
                                    if (st - 4) % 16 == 15:
                                        sy.wait_ge(sD_d, (st - 4) // 16 + 1)
                                    else:
                                        sy.wait_ge(sD_dcons, dcons_cnt(st - 3))
                                sy.dma_start(
                                    wdb[st % 4][:, :]
                                    .rearrange("p (c o) -> p c o", c=4),
                                    wd_bf[m * 512:(m + 1) * 512,
                                          ob * 512:(ob + 1) * 512]
                                    .rearrange("(c p) o -> p c o", p=128))\
                                    .then_inc(sD_dwd[st % 4], 16)
                            if ob == 0 and g + 1 < NG:
                                x2t_loads(g + 1, range(3, 4))
                            for tc in range(4):
                                oidx = (g * 4 + ob) * 4 + tc
                                tg = g * 4 + tc
                                if oidx >= 2:
                                    sy.wait_ge(sD_ev, oidx - 1)
                                sy.dma_start(
                                    xo_d[oidx % 2][:, :],
                                    x2_hbm[tg * 128:(tg + 1) * 128,
                                           ob * 512:(ob + 1) * 512])\
                                    .then_inc(sD_xo[oidx % 2], 16)

                @blk.gpsimd
                def _(g):
                    for gg in range(NG):
                        if debug and gg == 1:
                            g.wait_ge(sD_d, 4)
                            g.dma_start(dbg_ln2fm[:, :], ln2fm[:, :])\
                                .then_inc(sD_dbg, 16)
                            g.dma_start(dbg_gated[:, :], gated[:, :])\
                                .then_inc(sD_dbg, 16)
                            g.wait_ge(sD_dbg, 32)
                        for hh in range(64):
                            ghh = gg * 64 + hh
                            if ghh >= 2:
                                g.wait_ge(sD_u, ghh - 1)
                            g.dma_start(wgb[ghh % 2][:, :],
                                        wg_bf[hh * 128:(hh + 1) * 128, :])\
                                .then_inc(sD_wgd[ghh % 2], 16)
                            g.dma_start(wub[ghh % 2][:, :],
                                        wu_bf[hh * 128:(hh + 1) * 128, :])\
                                .then_inc(sD_wgd[ghh % 2], 16)

                @blk.vector
                def _(v):
                    def ln2_all(gx):
                        for tt in range(4):
                            ti = gx * 4 + tt
                            v.wait_ge(sD_x2[ti % 4], 16 * (ti // 4 + 1))
                            for q in range(4):
                                ins = v.bn_stats(
                                    bnst2[tt % 2][:, 6 * q:6 * q + 6],
                                    x2t[ti % 4][:, 512 * q:512 * (q + 1)])
                            vhop(v, ins)
                            vhop(v, v.bn_aggr(stat2[tt][:, :],
                                              bnst2[tt % 2][:, :]))
                            v.tensor_scalar_add(veps2[tt][:, 0:1],
                                                stat2[tt][:, 1:2], EPS)\
                                .then_inc(sD_var, 1)
                        for tt in range(4):
                            ti = gx * 4 + tt
                            v.wait_ge(sD_std, ti + 1)
                            vhop(v, v.reciprocal(rstd2[tt][:, 0:1],
                                                 stdv2[tt][:, 0:1]))
                            if ti >= 4:
                                v.wait_ge(sD_tg, 4 * (ti - 3))
                            v.tensor_scalar(ln2tm[tt][:, :],
                                            x2t[ti % 4][:, :],
                                            stat2[tt][:, 0:1],
                                            rstd2[tt][:, 0:1],
                                            ALU.subtract, ALU.mult)\
                                .then_inc(sD_ln, 1)

                    ln2_all(0)
                    for g in range(NG):
                        for hh in range(64):
                            ghh = g * 64 + hh
                            v.wait_ge(sD_si, ghh + 1)
                            v.wait_ge(sD_u, ghh + 1)
                            v.tensor_tensor(gated[:, hh * 512:(hh + 1) * 512],
                                            silu_sb[ghh % 2][:, :], u_ps[:, :],
                                            ALU.mult).then_inc(sD_mu, 1)
                        if g + 1 < NG:
                            ln2_all(g + 1)
                        for ob in range(4):
                            v.wait_ge(sD_d, g * 4 + ob + 1)
                            for tc in range(4):
                                oidx = (g * 4 + ob) * 4 + tc
                                if oidx >= 2:
                                    v.wait_ge(sD_od[oidx % 2], 16 * (oidx // 2))
                                v.wait_ge(sD_xo[oidx % 2], 16 * (oidx // 2 + 1))
                                v.tensor_tensor(
                                    outo[oidx % 2][:, :], d_ps[tc][:, :],
                                    xo_d[oidx % 2][:, :],
                                    ALU.add).then_inc(sD_ev, 1)

                @blk.scalar
                def _(s):
                    def sqrt4(gx):
                        for tt in range(4):
                            ti = gx * 4 + tt
                            s.wait_ge(sD_var, ti + 1)
                            if ti >= 4:
                                s.wait_ge(sD_ln, ti - 3)
                            s.activation(stdv2[tt][:, 0:1],
                                         veps2[tt][:, 0:1], AF.Sqrt)\
                                .then_inc(sD_std, 1)

                    def tevacs(gx):
                        for tt in range(4):
                            for gr in range(4):
                                gi = 4 * (gx * 4 + tt) + gr
                                s.wait_ge(sD_tg, gi + 1)
                                for c in range(4):
                                    kc = 4 * gr + c
                                    ins = s.activation(
                                        ln2fm[:, kc * 512 + tt * 128:
                                              kc * 512 + (tt + 1) * 128],
                                        tr2_ps[gi % 2][:, c * 128:(c + 1) * 128],
                                        AF.Copy)
                                    if c == 3:
                                        ins.then_inc(sD_tev, 1)

                    sqrt4(0)
                    tevacs(0)
                    for g in range(NG):
                        for hh in range(64):
                            ghh = g * 64 + hh
                            s.wait_ge(sD_g, ghh + 1)
                            if ghh >= 2:
                                s.wait_ge(sD_mu, ghh - 1)
                            s.activation(silu_sb[ghh % 2][:, :], g_ps[:, :],
                                         AF.Silu).then_inc(sD_si, 1)
                        if g + 1 < NG:
                            sqrt4(g + 1)
                        for ob in range(4):
                            for tc in range(4):
                                oidx = (g * 4 + ob) * 4 + tc
                                tg = g * 4 + tc
                                s.wait_ge(sD_ev, oidx + 1)
                                s.dma_start(out[tg * 128:(tg + 1) * 128,
                                                ob * 512:(ob + 1) * 512],
                                            outo[oidx % 2][:, :])\
                                    .then_inc(sD_od[oidx % 2], 16)
                        if g + 1 < NG:
                            tevacs(g + 1)
                    s.wait_ge(sD_od[0], 16 * 64)
                    s.wait_ge(sD_od[1], 16 * 64)

                @blk.tensor
                def _(t):
                    def tgroups(gx):
                        for tt in range(4):
                            ti = gx * 4 + tt
                            t.wait_ge(sD_ln, ti + 1)
                            for gr in range(4):
                                gi = 4 * ti + gr
                                if gi >= 2:
                                    t.wait_ge(sD_tev, gi - 1)
                                for c in range(4):
                                    kc = 4 * gr + c
                                    ins = t.matmul(
                                        tr2_ps[gi % 2][:, c * 128:(c + 1) * 128],
                                        ln2tm[tt][:, kc * 128:(kc + 1) * 128],
                                        ident_sb[:, :], start=True, stop=True)
                                    if c == 3:
                                        ins.then_inc(sD_tg, 1)

                    tgroups(0)
                    for g in range(NG):
                        if debug and g == 1:
                            t.wait_ge(sD_dbg, 32)
                        t.wait_ge(sD_tev, 16 * (g + 1))
                        for hh in range(64):
                            ghh = g * 64 + hh
                            t.wait_ge(sD_wgd[ghh % 2], 32 * (ghh // 2 + 1))
                            t.wait_ge(sD_si, ghh)
                            for kc in range(16):
                                ins = t.matmul(g_ps[:, :],
                                               wgb[ghh % 2][:, kc * 128:(kc + 1) * 128],
                                               ln2fm[:, kc * 512:(kc + 1) * 512],
                                               start=(kc == 0), stop=(kc == 15))
                                if kc == 15:
                                    ins.then_inc(sD_g, 1)
                            t.wait_ge(sD_mu, ghh)
                            for kc in range(16):
                                ins = t.matmul(u_ps[:, :],
                                               wub[ghh % 2][:, kc * 128:(kc + 1) * 128],
                                               ln2fm[:, kc * 512:(kc + 1) * 512],
                                               start=(kc == 0), stop=(kc == 15))
                                if kc == 15:
                                    ins.then_inc(sD_u, 1)
                        for ob in range(4):
                            t.wait_ge(sD_ev, 4 * (g * 4 + ob))
                            for m in range(16):
                                st = (g * 4 + ob) * 16 + m
                                t.wait_ge(sD_dwd[st % 4], 16 * (st // 4 + 1))
                                t.wait_ge(sD_mu, g * 64 + 4 * m + 4)
                                wdb_st = wdb[st % 4]
                                for c in range(4):
                                    hh = 4 * m + c
                                    for tc in range(4):
                                        ins = t.matmul(
                                            d_ps[tc][:, :],
                                            gated[:, hh * 512 + tc * 128:
                                                  hh * 512 + (tc + 1) * 128],
                                            wdb_st[:, c * 512:(c + 1) * 512],
                                            start=(hh == 0), stop=(hh == 63))
                                        if hh == 63 and tc == 3:
                                            ins.then_inc(sD_d, 1)
                                        elif c == 3 and tc == 3:
                                            ins.then_inc(sD_dcons, 1)
                        if g + 1 < NG:
                            tgroups(g + 1)

    nc.finalize()
    return nc


_NC_CACHE = {}


def _get_nc(debug=False):
    key = bool(debug)
    if key not in _NC_CACHE:
        _NC_CACHE[key] = build_nc(debug=key)
    return _NC_CACHE[key]


def _prep_weights(Wq, Wk, Wv, Wo, w_gate, w_up, w_down):
    f32 = np.float32
    wq = np.ascontiguousarray(np.asarray(Wq, dtype=f32).T.reshape(2048, 16, 128)
                              .transpose(1, 0, 2)).reshape(16 * 2048, 128)
    wk = np.ascontiguousarray(np.asarray(Wk, dtype=f32).T.reshape(2048, 16, 128)
                              .transpose(1, 0, 2)).reshape(16 * 2048, 128)
    wv = np.ascontiguousarray(np.asarray(Wv, dtype=f32).T.reshape(2048, 4, 512)
                              .transpose(1, 0, 2)).reshape(4 * 2048, 512)
    wo = np.ascontiguousarray(np.asarray(Wo, dtype=f32).T.reshape(2048, 4, 512)
                              .transpose(1, 0, 2)).reshape(4 * 2048, 512)
    wg = np.ascontiguousarray(np.asarray(w_gate, dtype=f32).T.reshape(2048, 64, 128)
                              .transpose(1, 0, 2)).reshape(64 * 2048, 128)
    wu = np.ascontiguousarray(np.asarray(w_up, dtype=f32).T.reshape(2048, 64, 128)
                              .transpose(1, 0, 2)).reshape(64 * 2048, 128)
    wd = np.ascontiguousarray(np.asarray(w_down, dtype=f32).T)
    return wq, wk, wv, wo, wg, wu, wd


def kernel(L0_attn_x, L0_attn_Wq, L0_attn_Wk, L0_attn_Wv, L0_attn_Wo,
           L0_ffn_w_gate, L0_ffn_w_up, L0_ffn_w_down):
    x = np.ascontiguousarray(np.asarray(L0_attn_x, dtype=np.float32))
    wq, wk, wv, wo, wg, wu, wd = _prep_weights(
        L0_attn_Wq, L0_attn_Wk, L0_attn_Wv, L0_attn_Wo,
        L0_ffn_w_gate, L0_ffn_w_up, L0_ffn_w_down)
    masklT = np.tril(np.full((128, 128), -1e9, dtype=np.float32), k=-1)\
        .astype(ml_dtypes.bfloat16)
    ident = np.eye(128, dtype=ml_dtypes.bfloat16)

    nc = _get_nc(debug=False)
    in_maps = []
    for c in range(NCORES):
        in_maps.append({
            "x_in": x[2 * c:2 * c + 2].reshape(NSEQ * T, D),
            "wqT": wq, "wkT": wk, "wvT": wv, "woT": wo,
            "wgT": wg, "wuT": wu, "wdT": wd,
            "masklT_in": masklT, "ident_in": ident,
        })
    res = run_bass_kernel_spmd(nc, in_maps, core_ids=list(range(NCORES)))
    outs = [r["out"].reshape(NSEQ, T, D) for r in res.results]
    return np.concatenate(outs, axis=0)


# version 11
# speedup vs baseline: 1.0483x; 1.0483x over previous
import numpy as np
import ml_dtypes
from contextlib import ExitStack

import concourse.bass as bass
import concourse.mybir as mybir
from concourse.bass_utils import run_bass_kernel_spmd

F32 = mybir.dt.float32
BF16 = mybir.dt.bfloat16
AF = mybir.ActivationFunctionType
ALU = mybir.AluOpType
AX = mybir.AxisListType

NCORES = 8
NSEQ = 2
T = 2048
D = 2048
NH = 16
HD = 128
HID = 8192
NT = NSEQ * T // 128
NTS = T // 128
SCALE = 1.0 / float(np.sqrt(HD))
EPS = 1e-5
GT = 512
NG = NSEQ * T // GT


class Sem:
    def __init__(self, ctx, nc, name):
        self.h = ctx.enter_context(nc.semaphore(name))


def build_nc(debug=False):
    nc = bass.Bass("TRN2", target_bir_lowering=False)

    x_in = nc.dram_tensor("x_in", [NSEQ * T, D], F32, kind="ExternalInput")
    wqT = nc.dram_tensor("wqT", [16 * 2048, 128], F32, kind="ExternalInput")
    wkT = nc.dram_tensor("wkT", [16 * 2048, 128], F32, kind="ExternalInput")
    wvT = nc.dram_tensor("wvT", [4 * 2048, 512], F32, kind="ExternalInput")
    woT = nc.dram_tensor("woT", [4 * 2048, 512], F32, kind="ExternalInput")
    wgT = nc.dram_tensor("wgT", [64 * 2048, 128], F32, kind="ExternalInput")
    wuT = nc.dram_tensor("wuT", [64 * 2048, 128], F32, kind="ExternalInput")
    wdT = nc.dram_tensor("wdT", [8192, 2048], F32, kind="ExternalInput")
    masklT_in = nc.dram_tensor("masklT_in", [128, 128], BF16, kind="ExternalInput")
    ident_in = nc.dram_tensor("ident_in", [128, 128], BF16, kind="ExternalInput")

    skind = "ExternalOutput" if debug else "Internal"
    q_hbm = nc.dram_tensor("q_hbm", [128, NH * NSEQ * T], BF16, kind=skind)
    k_hbm = nc.dram_tensor("k_hbm", [128, NH * NSEQ * T], BF16, kind=skind)
    v_hbm = nc.dram_tensor("v_hbm", [NSEQ * NH * NTS * 128, 128], BF16, kind=skind)
    z_hbm = nc.dram_tensor("z_hbm", [128, NH * NSEQ * T], BF16, kind=skind)
    x2_hbm = nc.dram_tensor("x2_hbm", [NSEQ * T, D], F32, kind=skind)
    wg_bf = nc.dram_tensor("wg_bf", [64 * 128, 2048], BF16, kind=skind)
    wu_bf = nc.dram_tensor("wu_bf", [64 * 128, 2048], BF16, kind=skind)
    wd_bf = nc.dram_tensor("wd_bf", [8192, 2048], BF16, kind=skind)
    out = nc.dram_tensor("out", [NSEQ * T, D], F32, kind="ExternalOutput")
    if debug:
        dbg_ln2fm = nc.dram_tensor("dbg_ln2fm", [128, 16 * 512], BF16,
                                   kind="ExternalOutput")
        dbg_gated = nc.dram_tensor("dbg_gated", [128, 64 * 512], BF16,
                                   kind="ExternalOutput")

    with ExitStack() as top:
        sb = lambda n, s, d: top.enter_context(nc.sbuf_tensor(n, s, d))
        S = lambda n: Sem(top, nc, n).h

        sVself = S("sVself")
        vself_n = [0]

        def vhop(v, ins):
            vself_n[0] += 1
            ins.then_inc(sVself, 1)
            v.wait_ge(sVself, vself_n[0])

        masklT_sb = sb("masklT_sb", [128, 128], BF16)
        ident_sb = sb("ident_sb", [128, 128], BF16)

        phab = top.enter_context(ExitStack())
        ln_fm = phab.enter_context(nc.sbuf_tensor("ln_fm", [128, 16 * NT * 128], BF16))

        sA_x = [S(f"sA_x{p}") for p in range(4)]
        sA_var = S("sA_var"); sA_std = S("sA_std")
        sA_ln = S("sA_ln"); sA_tg = S("sA_tg"); sA_tev = S("sA_tev")
        sA_c = S("sA_c")

        with ExitStack() as ph:
            psb = lambda n, s, d: ph.enter_context(nc.sbuf_tensor(n, s, d))
            x_sb = [psb(f"xA{p}", [128, D], F32) for p in range(4)]
            ln_tm = [psb(f"lntmA{p}", [128, D], BF16) for p in range(4)]
            bnst = [psb(f"bnstA{p}", [128, 24], F32) for p in range(2)]
            stat = [psb(f"statA{p}", [128, 2], F32) for p in range(4)]
            veps = [psb(f"vepsA{p}", [128, 1], F32) for p in range(4)]
            stdv = [psb(f"stdA{p}", [128, 1], F32) for p in range(4)]
            rstd = [psb(f"rstdA{p}", [128, 1], F32) for p in range(4)]
            tr_ps = [ph.enter_context(nc.psum_tensor(f"trpsA{p}", [128, 512], F32))
                     for p in range(2)]

            with nc.Block() as blk:
                @blk.sync
                def _(sy):
                    sy.dma_start(masklT_sb[:, :], masklT_in[:, :]).then_inc(sA_c, 16)
                    sy.dma_start(ident_sb[:, :], ident_in[:, :]).then_inc(sA_c, 16)
                    for i in range(NT):
                        if i >= 4:
                            sy.wait_ge(sA_ln, i - 3)
                        sy.dma_start(x_sb[i % 4][:, :],
                                     x_in[i * 128:(i + 1) * 128, :])\
                            .then_inc(sA_x[i % 4], 16)

                @blk.vector
                def _(v):
                    for i in range(NT + 1):
                        if i < NT:
                            v.wait_ge(sA_x[i % 4], 16 * (i // 4 + 1))
                            for k in range(4):
                                ins = v.bn_stats(bnst[i % 2][:, 6 * k:6 * k + 6],
                                                 x_sb[i % 4][:, 512 * k:512 * k + 512])
                            vhop(v, ins)
                            vhop(v, v.bn_aggr(stat[i % 4][:, :], bnst[i % 2][:, :]))
                            v.tensor_scalar_add(veps[i % 4][:, 0:1],
                                                stat[i % 4][:, 1:2], EPS)\
                                .then_inc(sA_var, 1)
                        if i >= 1:
                            j = i - 1
                            v.wait_ge(sA_std, j + 1)
                            vhop(v, v.reciprocal(rstd[j % 4][:, 0:1],
                                                 stdv[j % 4][:, 0:1]))
                            if j >= 4:
                                v.wait_ge(sA_tg, 4 * (j - 3))
                            v.tensor_scalar(ln_tm[j % 4][:, :], x_sb[j % 4][:, :],
                                            stat[j % 4][:, 0:1], rstd[j % 4][:, 0:1],
                                            ALU.subtract, ALU.mult)\
                                .then_inc(sA_ln, 1)

                @blk.scalar
                def _(s):
                    for i in range(NT + 1):
                        if i < NT:
                            s.wait_ge(sA_var, i + 1)
                            if i >= 4:
                                s.wait_ge(sA_ln, i - 3)
                            s.activation(stdv[i % 4][:, 0:1], veps[i % 4][:, 0:1],
                                         AF.Sqrt).then_inc(sA_std, 1)
                        if i >= 1:
                            j = i - 1
                            for g in range(4):
                                gi = 4 * j + g
                                s.wait_ge(sA_tg, gi + 1)
                                for c in range(4):
                                    kc = 4 * g + c
                                    ins = s.activation(
                                        ln_fm[:, kc * (NT * 128) + j * 128:
                                              kc * (NT * 128) + (j + 1) * 128],
                                        tr_ps[gi % 2][:, c * 128:(c + 1) * 128],
                                        AF.Copy)
                                    if c == 3:
                                        ins.then_inc(sA_tev, 1)

                @blk.tensor
                def _(t):
                    t.wait_ge(sA_c, 32)
                    for j in range(NT):
                        t.wait_ge(sA_ln, j + 1)
                        for g in range(4):
                            gi = 4 * j + g
                            if gi >= 2:
                                t.wait_ge(sA_tev, gi - 1)
                            for c in range(4):
                                kc = 4 * g + c
                                ins = t.matmul(tr_ps[gi % 2][:, c * 128:(c + 1) * 128],
                                               ln_tm[j % 4][:, kc * 128:(kc + 1) * 128],
                                               ident_sb[:, :], start=True, stop=True)
                                if c == 3:
                                    ins.then_inc(sA_tg, 1)

                @blk.gpsimd
                def _(g):
                    g.wait_ge(sA_tev, 4 * NT)

        sB_wd = [S("sB_wd0"), S("sB_wd1")]
        sB_od = [S("sB_od0"), S("sB_od1")]
        sB_wc = S("sB_wc"); sB_mm = S("sB_mm")
        sB_ev = S("sB_ev"); sB_pe = S("sB_pe")

        with ExitStack() as ph:
            psb = lambda n, s, d: ph.enter_context(nc.sbuf_tensor(n, s, d))
            wb = [psb(f"wbB{p}", [128, 2048], BF16) for p in range(2)]
            ev = [psb(f"evB{p}", [128, 512], BF16) for p in range(2)]
            qkps = [ph.enter_context(nc.psum_tensor(f"qkpsB{p}", [128, 512], F32))
                    for p in range(2)]

            with nc.Block() as blk:
                @blk.gpsimd
                def _(g):
                    for it in range(32):
                        w, oc = divmod(it, 16)
                        src = (wqT if w == 0 else wkT)
                        if it >= 2:
                            g.wait_ge(sB_mm, 8 * (it - 1))
                        g.dma_start(
                            wb[it % 2][:, :].rearrange("p (kc h) -> p kc h", kc=16),
                            src[oc * 2048:(oc + 1) * 2048, :]
                            .rearrange("(kc p) h -> p kc h", p=128))\
                            .then_inc(sB_wd[it % 2], 16)

                @blk.tensor
                def _(t):
                    for it in range(32):
                        t.wait_ge(sB_wd[it % 2], 16 * (it // 2 + 1))
                        for gg in range(8):
                            j = it * 8 + gg
                            if j >= 2:
                                t.wait_ge(sB_ev, j - 1)
                            for kc in range(16):
                                ins = t.matmul(
                                    qkps[j % 2][:, :],
                                    wb[it % 2][:, kc * 128:(kc + 1) * 128],
                                    ln_fm[:, kc * (NT * 128) + gg * 512:
                                          kc * (NT * 128) + (gg + 1) * 512],
                                    start=(kc == 0), stop=(kc == 15))
                                if kc == 15:
                                    ins.then_inc(sB_mm, 1)

                @blk.vector
                def _(v):
                    for j in range(256):
                        mult = SCALE if j < 128 else 1.0
                        v.wait_ge(sB_mm, j + 1)
                        if j >= 2:
                            v.wait_ge(sB_od[j % 2], 16 * (j // 2))
                        v.tensor_scalar_mul(ev[j % 2][:, :],
                                            qkps[j % 2][:, :], mult)\
                            .then_inc(sB_ev, 1)

                @blk.scalar
                def _(s):
                    for j in range(256):
                        it, gg = divmod(j, 8)
                        w, oc = divmod(it, 16)
                        dst = (q_hbm if w == 0 else k_hbm)
                        s.wait_ge(sB_ev, j + 1)
                        s.dma_start(dst[:, oc * (NT * 128) + gg * 512:
                                        oc * (NT * 128) + (gg + 1) * 512],
                                    ev[j % 2][:, :]).then_inc(sB_od[j % 2], 16)
                    s.wait_ge(sB_od[0], 16 * 128)
                    s.wait_ge(sB_od[1], 16 * 128)

        sV_od = [S("sV_od0"), S("sV_od1")]
        sV_wd = [S("sV_wd0"), S("sV_wd1")]
        sV_wc = S("sV_wc"); sV_mm = S("sV_mm")
        sV_ev = S("sV_ev"); sV_pe = S("sV_pe")

        with ExitStack() as ph:
            psb = lambda n, s, d: ph.enter_context(nc.sbuf_tensor(n, s, d))
            vwb = [psb(f"vwbB{p}", [128, 8192], BF16) for p in range(2)]
            vev = [psb(f"vevB{p}", [128, 512], BF16) for p in range(2)]
            vps = [ph.enter_context(nc.psum_tensor(f"vpsB{p}", [128, 512], F32))
                   for p in range(2)]

            with nc.Block() as blk:
                @blk.gpsimd
                def _(g):
                    for ob in range(4):
                        if ob >= 2:
                            g.wait_ge(sV_mm, NT * (ob - 1))
                        g.dma_start(
                            vwb[ob % 2][:, :].rearrange("p (kc o) -> p kc o", kc=16),
                            wvT[ob * 2048:(ob + 1) * 2048, :]
                            .rearrange("(kc p) o -> p kc o", p=128))\
                            .then_inc(sV_wd[ob % 2], 16)

                @blk.tensor
                def _(t):
                    for ob in range(4):
                        t.wait_ge(sV_wd[ob % 2], 16 * (ob // 2 + 1))
                        for i in range(NT):
                            j = ob * NT + i
                            if j >= 2:
                                t.wait_ge(sV_ev, j - 1)
                            for kc in range(16):
                                ins = t.matmul(
                                    vps[j % 2][:, :],
                                    ln_fm[:, kc * (NT * 128) + i * 128:
                                          kc * (NT * 128) + (i + 1) * 128],
                                    vwb[ob % 2][:, kc * 512:(kc + 1) * 512],
                                    start=(kc == 0), stop=(kc == 15))
                                if kc == 15:
                                    ins.then_inc(sV_mm, 1)

                @blk.vector
                def _(v):
                    for j in range(4 * NT):
                        v.wait_ge(sV_mm, j + 1)
                        if j >= 2:
                            v.wait_ge(sV_od[j % 2], 64 * (j // 2))
                        v.tensor_copy(vev[j % 2][:, :], vps[j % 2][:, :])\
                            .then_inc(sV_ev, 1)

                @blk.scalar
                def _(s):
                    for j in range(4 * NT):
                        ob, i = divmod(j, NT)
                        sq, jt = divmod(i, NTS)
                        s.wait_ge(sV_ev, j + 1)
                        for hh in range(4):
                            h = ob * 4 + hh
                            r0 = ((sq * NH + h) * NTS + jt) * 128
                            s.dma_start(v_hbm[r0:r0 + 128, :],
                                        vev[j % 2][:, hh * 128:(hh + 1) * 128])\
                                .then_inc(sV_od[j % 2], 16)
                    s.wait_ge(sV_od[0], 64 * 64)
                    s.wait_ge(sV_od[1], 64 * 64)

        phab.close()

        sC_qkv = [S("sC_qkv0"), S("sC_qkv1")]
        sC_s = S("sC_s")
        sC_sevA = S("sC_sevA"); sC_sevB = S("sC_sevB")
        sC_negmx = S("sC_negmx"); sC_exp = S("sC_exp"); sC_pnorm = S("sC_pnorm")
        sC_tg = S("sC_tg"); sC_evA = S("sC_evA")
        sC_z = S("sC_z"); sC_zev = S("sC_zev")
        sC_zd = [S("sC_zd0"), S("sC_zd1")]
        sC_hdone = S("sC_hdone")
        sP_in = [S("sP_in0"), S("sP_in1")]
        sP_out = [S("sP_out0"), S("sP_out1")]

        CUMJ = [0, 4, 12, 24, 40]

        def cntB(r):
            hh_full, i_part = divmod(r, 16)
            return hh_full * 8 + max(0, i_part - 8)

        grp_tab = []
        row_first_g = []
        absg = 0
        for hh in range(32):
            for i in range(16):
                row = []
                row_first_g.append(absg)
                for js in range(0, i + 1, 4):
                    row.append((absg, js, min(4, i + 1 - js)))
                    absg += 1
                grp_tab.append(row)
        NGRP = absg

        with ExitStack() as ph:
            psb = lambda n, s, d: ph.enter_context(nc.sbuf_tensor(n, s, d))
            q_sb = [psb(f"qC{p}", [128, 2048], BF16) for p in range(2)]
            k_sb = [psb(f"kC{p}", [128, 2048], BF16) for p in range(2)]
            v_sb = [psb(f"vC{p}", [128, 2048], BF16) for p in range(2)]
            scores = [psb(f"scC{p}", [128, 2048], F32) for p in range(2)]
            p_sb = [psb(f"pC{p}", [128, 2048], BF16) for p in range(2)]
            pn_sb = [psb(f"pnC{p}", [128, 2048], BF16) for p in range(4)]
            pt_sb = psb("ptC", [128, 16 * 512], BF16)
            zh_sb = [psb(f"zhC{p}", [128, 2048], BF16) for p in range(2)]
            mx = [psb(f"mxC{p}", [128, 1], F32) for p in range(2)]
            mxb = [psb(f"mxbC{p}", [128, 1], F32) for p in range(2)]
            negmx = [psb(f"ngC{p}", [128, 1], F32) for p in range(2)]
            sm = [psb(f"smC{p}", [128, 1], F32) for p in range(2)]
            rc = [psb(f"rcC{p}", [128, 1], F32) for p in range(2)]
            s_ps = ph.enter_context(nc.psum_tensor("spsC", [128, 2048], F32))
            pt_ps = [ph.enter_context(nc.psum_tensor(f"ptpsC{p}", [128, 512], F32))
                     for p in range(2)]
            z_ps = [ph.enter_context(nc.psum_tensor(f"zpsC{p}", [128, 512], F32))
                    for p in range(2)]

            ptv = pt_sb[:, :].rearrange("p (j c) -> p j c", j=16)
            ptsl = pt_sb[:, :].rearrange("p (j s c) -> p j s c", j=16, s=4)

            def qkv_load(sy, hh):
                sq, h = divmod(hh, 16)
                par = hh % 2
                base = h * (NT * 128) + sq * T
                sy.dma_start(q_sb[par][:, :], q_hbm[:, base:base + T])\
                    .then_inc(sC_qkv[par], 16)
                sy.dma_start(k_sb[par][:, :], k_hbm[:, base:base + T])\
                    .then_inc(sC_qkv[par], 16)
                r0 = (sq * NH + h) * NTS * 128
                sy.dma_start(
                    v_sb[par][:, :].rearrange("p (j hd) -> p j hd", j=16),
                    v_hbm[r0:r0 + NTS * 128, :]
                    .rearrange("(j p) hd -> p j hd", p=128))\
                    .then_inc(sC_qkv[par], 16)

            with nc.Block() as blk:
                @blk.sync
                def _(sy):
                    qkv_load(sy, 0)
                    qkv_load(sy, 1)
                    for hh in range(1, 31):
                        sy.wait_ge(sC_z, 4 * hh)
                        qkv_load(sy, hh + 1)
                    sy.wait_ge(sC_zd[0], 16 * 16)
                    sy.wait_ge(sC_zd[1], 16 * 16)

                @blk.tensor
                def _(t):
                    for hh in range(32):
                        par = hh % 2
                        for k in range(19):
                            if k <= 15:
                                i = k; r = hh * 16 + i
                                if k == 0:
                                    t.wait_ge(sC_qkv[hh % 2], 48 * (hh // 2 + 1))
                                t.wait_ge(sC_sevA, r)
                                if i >= 8:
                                    t.wait_ge(sC_sevB, cntB(r))
                                tot = (i + 1) * 128
                                nch = (tot + 511) // 512
                                dch = i // 4
                                for c in range(nch):
                                    wc = min(512, tot - c * 512)
                                    t.matmul(
                                        s_ps[:, c * 512:c * 512 + wc],
                                        q_sb[par][:, i * 128:(i + 1) * 128],
                                        k_sb[par][:, c * 512:c * 512 + wc],
                                        start=True, stop=(c != dch))
                                t.matmul(s_ps[:, i * 128:(i + 1) * 128],
                                         masklT_sb[:, :], ident_sb[:, :],
                                         start=False, stop=True)\
                                    .then_inc(sC_s, 1)
                            if k >= 3 and (k - 3) % 4 == 3:
                                i2 = k - 3; r2 = hh * 16 + i2
                                blk = i2 // 4
                                t.wait_ge(sC_pnorm, r2 + 1)
                                nj = 4 * blk + 4
                                jgb = hh * 40 + CUMJ[blk]
                                for jj in range(nj):
                                    gidx = jgb + jj
                                    if gidx >= 2:
                                        t.wait_ge(sC_evA, gidx - 1)
                                    lo = max(0, jj - 4 * blk)
                                    for srow in range(lo, 4):
                                        ins = t.matmul(
                                            pt_ps[gidx % 2][:, (srow - lo) * 128:
                                                            (srow - lo + 1) * 128],
                                            pn_sb[srow][:, jj * 128:(jj + 1) * 128],
                                            ident_sb[:, :], start=True, stop=True)
                                        if srow == 3:
                                            ins.then_inc(sC_tg, 1)
                                zb = hh * 4 + blk
                                if zb >= 2:
                                    t.wait_ge(sC_zev, zb - 1)
                                for jj in range(nj):
                                    t.wait_ge(sC_evA, jgb + jj + 1)
                                    lo = max(0, jj - 4 * blk)
                                    ins = t.matmul(
                                        z_ps[blk % 2][:, lo * 128:512],
                                        v_sb[par][:, jj * 128:(jj + 1) * 128],
                                        ptv[:, jj, lo * 128:512],
                                        start=(jj == 0), stop=(jj == nj - 1))
                                    if jj == nj - 1:
                                        ins.then_inc(sC_z, 1)


                @blk.vector
                def _(v):
                    for hh in range(32):
                        for k in range(19):
                            if k <= 15:
                                i = k; r = hh * 16 + i
                                v.wait_ge(sC_s, r + 1)
                                if r >= 2:
                                    v.wait_ge(sC_exp, r - 1)
                                tot = (i + 1) * 128
                                wA = min(tot, 1024)
                                v.tensor_scalar(scores[i % 2][:, 0:wA],
                                                s_ps[:, 0:wA], 0.0, None,
                                                ALU.add, ALU.max,
                                                accum_out=mx[i % 2][:, 0:1])\
                                    .then_inc(sC_sevA, 1)
                                v.wait_ge(sC_sevA, r + 1)
                                if tot > 1024:
                                    v.tensor_scalar(scores[i % 2][:, 1024:tot],
                                                    s_ps[:, 1024:tot], 0.0, None,
                                                    ALU.add, ALU.max,
                                                    accum_out=mxb[i % 2][:, 0:1])\
                                        .then_inc(sC_sevB, 1)
                                    v.wait_ge(sC_sevB, cntB(r) + 1)
                                    v.tensor_max(mx[i % 2][:, 0:1],
                                                 mx[i % 2][:, 0:1],
                                                 mxb[i % 2][:, 0:1])
                                v.tensor_scalar_mul(negmx[i % 2][:, 0:1],
                                                    mx[i % 2][:, 0:1], -1.0)\
                                    .then_inc(sC_negmx, 1)
                            if 1 <= k <= 16:
                                i1 = k - 1; r1 = hh * 16 + i1
                                v.wait_ge(sC_exp, r1 + 1)
                                vhop(v, v.reciprocal(rc[i1 % 2][:, 0:1],
                                                     sm[i1 % 2][:, 0:1]))
                                if r1 >= 4:
                                    hh1, ii1 = divmod(r1, 16)
                                    v.wait_ge(sC_tg, hh1 * 40 + CUMJ[ii1 // 4])
                                tot = (i1 + 1) * 128
                                v.tensor_scalar(pn_sb[i1 % 4][:, 0:tot],
                                                p_sb[i1 % 2][:, 0:tot],
                                                rc[i1 % 2][:, 0:1], None, ALU.mult)\
                                    .then_inc(sC_pnorm, 1)

                @blk.scalar
                def _(s):
                    for hh in range(32):
                        for k in range(19):
                            if 1 <= k <= 16:
                                i1 = k - 1; r1 = hh * 16 + i1
                                s.wait_ge(sC_negmx, r1 + 1)
                                if r1 >= 2:
                                    s.wait_ge(sC_pnorm, r1 - 1)
                                tot = (i1 + 1) * 128
                                s.activation(p_sb[i1 % 2][:, 0:tot],
                                             scores[i1 % 2][:, 0:tot], AF.Exp,
                                             bias=negmx[i1 % 2][:, 0:1],
                                             accum_out=sm[i1 % 2][:, 0:1])\
                                    .then_inc(sC_exp, 1)
                            if k >= 3 and (k - 3) % 4 == 3:
                                i2 = k - 3; r2 = hh * 16 + i2
                                blk = i2 // 4
                                nj = 4 * blk + 4
                                jgb = hh * 40 + CUMJ[blk]
                                first = True
                                for jj in range(nj):
                                    gidx = jgb + jj
                                    s.wait_ge(sC_tg, gidx + 1)
                                    if first:
                                        s.wait_ge(sC_z, hh * 4 + blk)
                                        first = False
                                    lo = max(0, jj - 4 * blk)
                                    s.activation(
                                        pt_sb[:, jj * 512 + lo * 128:
                                              (jj + 1) * 512],
                                        pt_ps[gidx % 2][:, 0:(4 - lo) * 128],
                                        AF.Copy).then_inc(sC_evA, 1)
                                zb = hh * 4 + blk
                                s.wait_ge(sC_z, zb + 1)
                                if blk == 0 and hh >= 2:
                                    s.wait_ge(sC_zd[hh % 2], 16 * (hh // 2))
                                s.activation(
                                    zh_sb[hh % 2][:, blk * 512:(blk + 1) * 512],
                                    z_ps[blk % 2][:, :], AF.Copy)\
                                    .then_inc(sC_zev, 1)
                        sq, h = divmod(hh, 16)
                        s.wait_ge(sC_zev, 4 * (hh + 1))
                        s.dma_start(z_hbm[:, h * (NT * 128) + sq * T:
                                          h * (NT * 128) + sq * T + T],
                                    zh_sb[hh % 2][:, :]).then_inc(sC_zd[hh % 2], 16)
                    s.wait_ge(sC_zd[0], 16 * 16)
                    s.wait_ge(sC_zd[1], 16 * 16)

                @blk.gpsimd
                def _(g):
                    np_ = 0
                    for b in range(64):
                        g.dma_start(wg_bf[b * 128:(b + 1) * 128, :]
                                    .rearrange("p (kc h) -> p kc h", kc=16),
                                    wgT[b * 2048:(b + 1) * 2048, :]
                                    .rearrange("(kc p) h -> p kc h", p=128))\
                            .then_inc(sP_out[0], 16)
                        g.dma_start(wu_bf[b * 128:(b + 1) * 128, :]
                                    .rearrange("p (kc h) -> p kc h", kc=16),
                                    wuT[b * 2048:(b + 1) * 2048, :]
                                    .rearrange("(kc p) h -> p kc h", p=128))\
                            .then_inc(sP_out[0], 16)
                        g.dma_start(wd_bf[b * 128:(b + 1) * 128, :],
                                    wdT[b * 128:(b + 1) * 128, :])\
                            .then_inc(sP_out[0], 16)
                        np_ += 3
                    g.wait_ge(sP_out[0], 16 * np_)

        sW_zd = [S("sW_zd0"), S("sW_zd1"), S("sW_zd2")]
        sW_xd = [S("sW_xd0"), S("sW_xd1")]
        sW_x2d = [S("sW_x2d0"), S("sW_x2d1")]
        sW_wd = S("sW_wd"); sW_wc = S("sW_wc")
        sW_mm = S("sW_mm"); sW_mmg = S("sW_mmg")
        sW_ev = S("sW_ev"); sW_pe = S("sW_pe")

        with ExitStack() as ph:
            psb = lambda n, s, d: ph.enter_context(nc.sbuf_tensor(n, s, d))
            c2wb = [psb(f"c2wb{p}", [128, 8192], BF16) for p in range(2)]
            zg_sb = [psb(f"zgC2{p}", [128, 8192], BF16) for p in range(3)]
            xo_sb = [psb(f"xoC2{p}", [128, 512], F32) for p in range(2)]
            x2o = [psb(f"x2oC2{p}", [128, 512], F32) for p in range(2)]
            c2ps = [ph.enter_context(nc.psum_tensor(f"c2ps{p}", [128, 512], F32))
                    for p in range(2)]

            with nc.Block() as blk:
                @blk.gpsimd
                def _(g):
                    for ob in range(4):
                        if ob >= 2:
                            g.wait_ge(sW_mm, 32 * (ob - 1))
                        g.dma_start(
                            c2wb[ob % 2][:, :].rearrange("p (dc o) -> p dc o", dc=16),
                            woT[ob * 2048:(ob + 1) * 2048, :]
                            .rearrange("(dc p) o -> p dc o", p=128))\
                            .then_inc(sW_wd, 16)

                @blk.sync
                def _(sy):
                    for ob in range(4):
                        for grp in range(NG):
                            g_it = ob * NG + grp
                            if g_it >= 3:
                                sy.wait_ge(sW_mm, 4 * (g_it - 2))
                            sy.dma_start(
                                zg_sb[g_it % 3][:, :]
                                .rearrange("p (dc c) -> p dc c", dc=16),
                                z_hbm[:, :]
                                .rearrange("p (dc t) -> p dc t", dc=16)
                                [:, :, grp * 512:(grp + 1) * 512])\
                                .then_inc(sW_zd[g_it % 3], 16)
                            for tc in range(4):
                                it = g_it * 4 + tc
                                tg = grp * 4 + tc
                                if it >= 2:
                                    sy.wait_ge(sW_ev, it - 1)
                                sy.dma_start(
                                    xo_sb[it % 2][:, :],
                                    x_in[tg * 128:(tg + 1) * 128,
                                         ob * 512:(ob + 1) * 512])\
                                    .then_inc(sW_xd[it % 2], 16)

                @blk.tensor
                def _(t):
                    for ob in range(4):
                        t.wait_ge(sW_wd, 16 * (ob + 1))
                        for grp in range(NG):
                            g_it = ob * NG + grp
                            t.wait_ge(sW_zd[g_it % 3], 16 * (g_it // 3 + 1))
                            for tc in range(4):
                                it = g_it * 4 + tc
                                if it >= 2:
                                    t.wait_ge(sW_ev, it - 1)
                                for dc in range(16):
                                    ins = t.matmul(
                                        c2ps[it % 2][:, :],
                                        zg_sb[g_it % 3][:, dc * 512 + tc * 128:
                                                        dc * 512 + (tc + 1) * 128],
                                        c2wb[ob % 2][:, dc * 512:(dc + 1) * 512],
                                        start=(dc == 0), stop=(dc == 15))
                                    if dc == 15:
                                        ins.then_inc(sW_mm, 1)

                @blk.vector
                def _(v):
                    for it in range(128):
                        v.wait_ge(sW_mm, it + 1)
                        v.wait_ge(sW_xd[it % 2], 16 * (it // 2 + 1))
                        if it >= 2:
                            v.wait_ge(sW_x2d[it % 2], 16 * (it // 2))
                        v.tensor_tensor(x2o[it % 2][:, :], c2ps[it % 2][:, :],
                                        xo_sb[it % 2][:, :], ALU.add)\
                            .then_inc(sW_ev, 1)

                @blk.scalar
                def _(s):
                    for it in range(128):
                        g_it, tc = divmod(it, 4)
                        ob, grp = divmod(g_it, NG)
                        tg = grp * 4 + tc
                        s.wait_ge(sW_ev, it + 1)
                        s.dma_start(x2_hbm[tg * 128:(tg + 1) * 128,
                                           ob * 512:(ob + 1) * 512],
                                    x2o[it % 2][:, :]).then_inc(sW_x2d[it % 2], 16)
                    s.wait_ge(sW_x2d[0], 16 * 64)
                    s.wait_ge(sW_x2d[1], 16 * 64)

        sD_x2 = [S(f"sD_x2{p}") for p in range(4)]
        sD_wgd = [S("sD_wgd0"), S("sD_wgd1")]
        sD_dwd = [S(f"sD_dwd{p}") for p in range(4)]
        sD_od = [S("sD_od0"), S("sD_od1")]
        sD_var = S("sD_var"); sD_std = S("sD_std")
        sD_ln = S("sD_ln"); sD_tg = S("sD_tg"); sD_tev = S("sD_tev")
        sD_gu = S("sD_gu")
        sD_g = S("sD_g"); sD_u = S("sD_u"); sD_si = S("sD_si"); sD_mu = S("sD_mu")
        sD_dcons = S("sD_dcons"); sD_d = S("sD_d")
        sD_ev = S("sD_ev"); sD_dbg = S("sD_dbg")

        sD_xo = [S("sD_xo0"), S("sD_xo1")]

        with ExitStack() as ph:
            psb = lambda n, s, d: ph.enter_context(nc.sbuf_tensor(n, s, d))
            x2t = [psb(f"x2tD{p}", [128, D], F32) for p in range(4)]
            xo_d = [psb(f"xoD{p}", [128, 512], F32) for p in range(2)]
            ln2fm = psb("ln2fmD", [128, 16 * 512], BF16)
            gated = psb("gatedD", [128, 64 * 512], BF16)
            wgb = [psb(f"wgbD{p}", [128, 2048], BF16) for p in range(2)]
            wub = [psb(f"wubD{p}", [128, 2048], BF16) for p in range(2)]
            wdb = [psb(f"wdbD{p}", [128, 2048], BF16) for p in range(4)]
            silu_sb = [psb(f"siluD{p}", [128, 512], F32) for p in range(2)]
            outo = [psb(f"outoD{p}", [128, 512], F32) for p in range(2)]
            ln2tm = [psb(f"ln2tmD{p}", [128, D], BF16) for p in range(4)]
            bnst2 = [psb(f"bnstD{p}", [128, 24], F32) for p in range(2)]
            stat2 = [psb(f"statD{p}", [128, 2], F32) for p in range(4)]
            veps2 = [psb(f"vepsD{p}", [128, 1], F32) for p in range(4)]
            stdv2 = [psb(f"stdD{p}", [128, 1], F32) for p in range(4)]
            rstd2 = [psb(f"rstdD{p}", [128, 1], F32) for p in range(4)]
            g_ps = ph.enter_context(nc.psum_tensor("gpsD", [128, 512], F32))
            u_ps = ph.enter_context(nc.psum_tensor("upsD", [128, 512], F32))
            d_ps = [ph.enter_context(nc.psum_tensor(f"dpsD{p}", [128, 512], F32))
                    for p in range(4)]
            tr2_ps = [ph.enter_context(nc.psum_tensor(f"tr2D{p}", [128, 512], F32))
                      for p in range(2)]

            with nc.Block() as blk:
                @blk.sync
                def _(sy):
                    def x2t_loads(gx, rng=range(4)):
                        for tt in rng:
                            ti = gx * 4 + tt
                            if ti >= 4:
                                sy.wait_ge(sD_ln, ti - 3)
                            sy.dma_start(
                                x2t[ti % 4][:, :],
                                x2_hbm[ti * 128:(ti + 1) * 128, :])\
                                .then_inc(sD_x2[ti % 4], 16)

                    for g in range(NG):
                        if g == 0:
                            x2t_loads(0)
                            x2t_loads(1, range(3))
                        elif g + 1 < NG:
                            x2t_loads(g + 1, range(3))
                        def dcons_cnt(upto):
                            return upto - upto // 16

                        for ob in range(4):
                            for m in range(16):
                                st = (g * 4 + ob) * 16 + m
                                if st >= 4:
                                    if (st - 4) % 16 == 15:
                                        sy.wait_ge(sD_d, (st - 4) // 16 + 1)
                                    else:
                                        sy.wait_ge(sD_dcons, dcons_cnt(st - 3))
                                sy.dma_start(
                                    wdb[st % 4][:, :]
                                    .rearrange("p (c o) -> p c o", c=4),
                                    wd_bf[m * 512:(m + 1) * 512,
                                          ob * 512:(ob + 1) * 512]
                                    .rearrange("(c p) o -> p c o", p=128))\
                                    .then_inc(sD_dwd[st % 4], 16)
                            if ob == 0 and g + 1 < NG:
                                x2t_loads(g + 1, range(3, 4))
                            for tc in range(4):
                                oidx = (g * 4 + ob) * 4 + tc
                                tg = g * 4 + tc
                                if oidx >= 2:
                                    sy.wait_ge(sD_ev, oidx - 1)
                                sy.dma_start(
                                    xo_d[oidx % 2][:, :],
                                    x2_hbm[tg * 128:(tg + 1) * 128,
                                           ob * 512:(ob + 1) * 512])\
                                    .then_inc(sD_xo[oidx % 2], 16)

                @blk.gpsimd
                def _(g):
                    for gg in range(NG):
                        if debug and gg == 1:
                            g.wait_ge(sD_d, 4)
                            g.dma_start(dbg_ln2fm[:, :], ln2fm[:, :])\
                                .then_inc(sD_dbg, 16)
                            g.dma_start(dbg_gated[:, :], gated[:, :])\
                                .then_inc(sD_dbg, 16)
                            g.wait_ge(sD_dbg, 32)
                        for hh in range(64):
                            ghh = gg * 64 + hh
                            if ghh >= 2:
                                g.wait_ge(sD_u, ghh - 1)
                            g.dma_start(wgb[ghh % 2][:, :],
                                        wg_bf[hh * 128:(hh + 1) * 128, :])\
                                .then_inc(sD_wgd[ghh % 2], 16)
                            g.dma_start(wub[ghh % 2][:, :],
                                        wu_bf[hh * 128:(hh + 1) * 128, :])\
                                .then_inc(sD_wgd[ghh % 2], 16)

                @blk.vector
                def _(v):
                    def ln2_all(gx):
                        for tt in range(4):
                            ti = gx * 4 + tt
                            v.wait_ge(sD_x2[ti % 4], 16 * (ti // 4 + 1))
                            for q in range(4):
                                ins = v.bn_stats(
                                    bnst2[tt % 2][:, 6 * q:6 * q + 6],
                                    x2t[ti % 4][:, 512 * q:512 * (q + 1)])
                            vhop(v, ins)
                            vhop(v, v.bn_aggr(stat2[tt][:, :],
                                              bnst2[tt % 2][:, :]))
                            v.tensor_scalar_add(veps2[tt][:, 0:1],
                                                stat2[tt][:, 1:2], EPS)\
                                .then_inc(sD_var, 1)
                        for tt in range(4):
                            ti = gx * 4 + tt
                            v.wait_ge(sD_std, ti + 1)
                            vhop(v, v.reciprocal(rstd2[tt][:, 0:1],
                                                 stdv2[tt][:, 0:1]))
                            if ti >= 4:
                                v.wait_ge(sD_tg, 4 * (ti - 3))
                            v.tensor_scalar(ln2tm[tt][:, :],
                                            x2t[ti % 4][:, :],
                                            stat2[tt][:, 0:1],
                                            rstd2[tt][:, 0:1],
                                            ALU.subtract, ALU.mult)\
                                .then_inc(sD_ln, 1)

                    ln2_all(0)
                    for g in range(NG):
                        for hh in range(64):
                            ghh = g * 64 + hh
                            v.wait_ge(sD_si, ghh + 1)
                            v.wait_ge(sD_u, ghh + 1)
                            v.tensor_tensor(gated[:, hh * 512:(hh + 1) * 512],
                                            silu_sb[ghh % 2][:, :], u_ps[:, :],
                                            ALU.mult).then_inc(sD_mu, 1)
                        if g + 1 < NG:
                            ln2_all(g + 1)
                        for ob in range(4):
                            v.wait_ge(sD_d, g * 4 + ob + 1)
                            for tc in range(4):
                                oidx = (g * 4 + ob) * 4 + tc
                                if oidx >= 2:
                                    v.wait_ge(sD_od[oidx % 2], 16 * (oidx // 2))
                                v.wait_ge(sD_xo[oidx % 2], 16 * (oidx // 2 + 1))
                                v.tensor_tensor(
                                    outo[oidx % 2][:, :], d_ps[tc][:, :],
                                    xo_d[oidx % 2][:, :],
                                    ALU.add).then_inc(sD_ev, 1)

                @blk.scalar
                def _(s):
                    def sqrt4(gx):
                        for tt in range(4):
                            ti = gx * 4 + tt
                            s.wait_ge(sD_var, ti + 1)
                            if ti >= 4:
                                s.wait_ge(sD_ln, ti - 3)
                            s.activation(stdv2[tt][:, 0:1],
                                         veps2[tt][:, 0:1], AF.Sqrt)\
                                .then_inc(sD_std, 1)

                    def tevacs(gx):
                        for tt in range(4):
                            for gr in range(4):
                                gi = 4 * (gx * 4 + tt) + gr
                                s.wait_ge(sD_tg, gi + 1)
                                for c in range(4):
                                    kc = 4 * gr + c
                                    ins = s.activation(
                                        ln2fm[:, kc * 512 + tt * 128:
                                              kc * 512 + (tt + 1) * 128],
                                        tr2_ps[gi % 2][:, c * 128:(c + 1) * 128],
                                        AF.Copy)
                                    if c == 3:
                                        ins.then_inc(sD_tev, 1)

                    sqrt4(0)
                    tevacs(0)
                    for g in range(NG):
                        for hh in range(64):
                            ghh = g * 64 + hh
                            s.wait_ge(sD_g, ghh + 1)
                            if ghh >= 2:
                                s.wait_ge(sD_mu, ghh - 1)
                            s.activation(silu_sb[ghh % 2][:, :], g_ps[:, :],
                                         AF.Silu).then_inc(sD_si, 1)
                        if g + 1 < NG:
                            sqrt4(g + 1)
                        for ob in range(4):
                            for tc in range(4):
                                oidx = (g * 4 + ob) * 4 + tc
                                tg = g * 4 + tc
                                s.wait_ge(sD_ev, oidx + 1)
                                s.dma_start(out[tg * 128:(tg + 1) * 128,
                                                ob * 512:(ob + 1) * 512],
                                            outo[oidx % 2][:, :])\
                                    .then_inc(sD_od[oidx % 2], 16)
                        if g + 1 < NG:
                            tevacs(g + 1)
                    s.wait_ge(sD_od[0], 16 * 64)
                    s.wait_ge(sD_od[1], 16 * 64)

                @blk.tensor
                def _(t):
                    def tgroups(gx):
                        for tt in range(4):
                            ti = gx * 4 + tt
                            t.wait_ge(sD_ln, ti + 1)
                            for gr in range(4):
                                gi = 4 * ti + gr
                                if gi >= 2:
                                    t.wait_ge(sD_tev, gi - 1)
                                for c in range(4):
                                    kc = 4 * gr + c
                                    ins = t.matmul(
                                        tr2_ps[gi % 2][:, c * 128:(c + 1) * 128],
                                        ln2tm[tt][:, kc * 128:(kc + 1) * 128],
                                        ident_sb[:, :], start=True, stop=True)
                                    if c == 3:
                                        ins.then_inc(sD_tg, 1)

                    tgroups(0)
                    for g in range(NG):
                        if debug and g == 1:
                            t.wait_ge(sD_dbg, 32)
                        t.wait_ge(sD_tev, 16 * (g + 1))
                        for hh in range(64):
                            ghh = g * 64 + hh
                            t.wait_ge(sD_wgd[ghh % 2], 32 * (ghh // 2 + 1))
                            t.wait_ge(sD_si, ghh)
                            for kc in range(16):
                                ins = t.matmul(g_ps[:, :],
                                               wgb[ghh % 2][:, kc * 128:(kc + 1) * 128],
                                               ln2fm[:, kc * 512:(kc + 1) * 512],
                                               start=(kc == 0), stop=(kc == 15))
                                if kc == 15:
                                    ins.then_inc(sD_g, 1)
                            t.wait_ge(sD_mu, ghh)
                            for kc in range(16):
                                ins = t.matmul(u_ps[:, :],
                                               wub[ghh % 2][:, kc * 128:(kc + 1) * 128],
                                               ln2fm[:, kc * 512:(kc + 1) * 512],
                                               start=(kc == 0), stop=(kc == 15))
                                if kc == 15:
                                    ins.then_inc(sD_u, 1)
                        for ob in range(4):
                            t.wait_ge(sD_ev, 4 * (g * 4 + ob))
                            for m in range(16):
                                st = (g * 4 + ob) * 16 + m
                                t.wait_ge(sD_dwd[st % 4], 16 * (st // 4 + 1))
                                t.wait_ge(sD_mu, g * 64 + 4 * m + 4)
                                wdb_st = wdb[st % 4]
                                for c in range(4):
                                    hh = 4 * m + c
                                    for tc in range(4):
                                        ins = t.matmul(
                                            d_ps[tc][:, :],
                                            gated[:, hh * 512 + tc * 128:
                                                  hh * 512 + (tc + 1) * 128],
                                            wdb_st[:, c * 512:(c + 1) * 512],
                                            start=(hh == 0), stop=(hh == 63))
                                        if hh == 63 and tc == 3:
                                            ins.then_inc(sD_d, 1)
                                        elif c == 3 and tc == 3:
                                            ins.then_inc(sD_dcons, 1)
                        if g + 1 < NG:
                            tgroups(g + 1)

    nc.finalize()
    return nc


_NC_CACHE = {}


def _get_nc(debug=False):
    key = bool(debug)
    if key not in _NC_CACHE:
        _NC_CACHE[key] = build_nc(debug=key)
    return _NC_CACHE[key]


def _prep_weights(Wq, Wk, Wv, Wo, w_gate, w_up, w_down):
    f32 = np.float32
    wq = np.ascontiguousarray(np.asarray(Wq, dtype=f32).T.reshape(2048, 16, 128)
                              .transpose(1, 0, 2)).reshape(16 * 2048, 128)
    wk = np.ascontiguousarray(np.asarray(Wk, dtype=f32).T.reshape(2048, 16, 128)
                              .transpose(1, 0, 2)).reshape(16 * 2048, 128)
    wv = np.ascontiguousarray(np.asarray(Wv, dtype=f32).T.reshape(2048, 4, 512)
                              .transpose(1, 0, 2)).reshape(4 * 2048, 512)
    wo = np.ascontiguousarray(np.asarray(Wo, dtype=f32).T.reshape(2048, 4, 512)
                              .transpose(1, 0, 2)).reshape(4 * 2048, 512)
    wg = np.ascontiguousarray(np.asarray(w_gate, dtype=f32).T.reshape(2048, 64, 128)
                              .transpose(1, 0, 2)).reshape(64 * 2048, 128)
    wu = np.ascontiguousarray(np.asarray(w_up, dtype=f32).T.reshape(2048, 64, 128)
                              .transpose(1, 0, 2)).reshape(64 * 2048, 128)
    wd = np.ascontiguousarray(np.asarray(w_down, dtype=f32).T)
    return wq, wk, wv, wo, wg, wu, wd


def kernel(L0_attn_x, L0_attn_Wq, L0_attn_Wk, L0_attn_Wv, L0_attn_Wo,
           L0_ffn_w_gate, L0_ffn_w_up, L0_ffn_w_down):
    x = np.ascontiguousarray(np.asarray(L0_attn_x, dtype=np.float32))
    wq, wk, wv, wo, wg, wu, wd = _prep_weights(
        L0_attn_Wq, L0_attn_Wk, L0_attn_Wv, L0_attn_Wo,
        L0_ffn_w_gate, L0_ffn_w_up, L0_ffn_w_down)
    masklT = np.tril(np.full((128, 128), -1e9, dtype=np.float32), k=-1)\
        .astype(ml_dtypes.bfloat16)
    ident = np.eye(128, dtype=ml_dtypes.bfloat16)

    nc = _get_nc(debug=False)
    in_maps = []
    for c in range(NCORES):
        in_maps.append({
            "x_in": x[2 * c:2 * c + 2].reshape(NSEQ * T, D),
            "wqT": wq, "wkT": wk, "wvT": wv, "woT": wo,
            "wgT": wg, "wuT": wu, "wdT": wd,
            "masklT_in": masklT, "ident_in": ident,
        })
    res = run_bass_kernel_spmd(nc, in_maps, core_ids=list(range(NCORES)))
    outs = [r["out"].reshape(NSEQ, T, D) for r in res.results]
    return np.concatenate(outs, axis=0)


# version 12
# speedup vs baseline: 1.1131x; 1.0618x over previous
import numpy as np
import ml_dtypes
from contextlib import ExitStack

import concourse.bass as bass
import concourse.mybir as mybir
from concourse.bass_utils import run_bass_kernel_spmd

F32 = mybir.dt.float32
BF16 = mybir.dt.bfloat16
AF = mybir.ActivationFunctionType
ALU = mybir.AluOpType
AX = mybir.AxisListType

NCORES = 8
NSEQ = 2
T = 2048
D = 2048
NH = 16
HD = 128
HID = 8192
NT = NSEQ * T // 128
NTS = T // 128
SCALE = 1.0 / float(np.sqrt(HD))
EPS = 1e-5
GT = 512
NG = NSEQ * T // GT


class Sem:
    def __init__(self, ctx, nc, name):
        self.h = ctx.enter_context(nc.semaphore(name))


def build_nc(debug=False):
    nc = bass.Bass("TRN2", target_bir_lowering=False)

    x_in = nc.dram_tensor("x_in", [NSEQ * T, D], F32, kind="ExternalInput")
    wqT = nc.dram_tensor("wqT", [16 * 2048, 128], F32, kind="ExternalInput")
    wkT = nc.dram_tensor("wkT", [16 * 2048, 128], F32, kind="ExternalInput")
    wvT = nc.dram_tensor("wvT", [4 * 2048, 512], F32, kind="ExternalInput")
    woT = nc.dram_tensor("woT", [4 * 2048, 512], F32, kind="ExternalInput")
    wgT = nc.dram_tensor("wgT", [64 * 2048, 128], F32, kind="ExternalInput")
    wuT = nc.dram_tensor("wuT", [64 * 2048, 128], F32, kind="ExternalInput")
    wdT = nc.dram_tensor("wdT", [8192, 2048], F32, kind="ExternalInput")
    masklT_in = nc.dram_tensor("masklT_in", [128, 128], BF16, kind="ExternalInput")
    ident_in = nc.dram_tensor("ident_in", [128, 128], BF16, kind="ExternalInput")

    skind = "ExternalOutput" if debug else "Internal"
    q_hbm = nc.dram_tensor("q_hbm", [128, NH * NSEQ * T], BF16, kind=skind)
    k_hbm = nc.dram_tensor("k_hbm", [128, NH * NSEQ * T], BF16, kind=skind)
    v_hbm = nc.dram_tensor("v_hbm", [NSEQ * NH * NTS * 128, 128], BF16, kind=skind)
    z_hbm = nc.dram_tensor("z_hbm", [128, NH * NSEQ * T], BF16, kind=skind)
    x2_hbm = nc.dram_tensor("x2_hbm", [NSEQ * T, D], F32, kind=skind)
    wg_bf = nc.dram_tensor("wg_bf", [64 * 128, 2048], BF16, kind=skind)
    wu_bf = nc.dram_tensor("wu_bf", [64 * 128, 2048], BF16, kind=skind)
    wd_bf = nc.dram_tensor("wd_bf", [8192, 2048], BF16, kind=skind)
    out = nc.dram_tensor("out", [NSEQ * T, D], F32, kind="ExternalOutput")
    if debug:
        dbg_ln2fm = nc.dram_tensor("dbg_ln2fm", [128, 16 * 512], BF16,
                                   kind="ExternalOutput")
        dbg_gated = nc.dram_tensor("dbg_gated", [128, 64 * 512], BF16,
                                   kind="ExternalOutput")

    with ExitStack() as top:
        sb = lambda n, s, d: top.enter_context(nc.sbuf_tensor(n, s, d))
        S = lambda n: Sem(top, nc, n).h

        sVself = S("sVself")
        vself_n = [0]

        def vhop(v, ins):
            vself_n[0] += 1
            ins.then_inc(sVself, 1)
            v.wait_ge(sVself, vself_n[0])

        masklT_sb = sb("masklT_sb", [128, 128], BF16)
        ident_sb = sb("ident_sb", [128, 128], BF16)

        phab = top.enter_context(ExitStack())
        ln_fm = phab.enter_context(nc.sbuf_tensor("ln_fm", [128, 16 * NT * 128], BF16))

        sA_x = [S(f"sA_x{p}") for p in range(4)]
        sA_var = S("sA_var"); sA_std = S("sA_std")
        sA_ln = S("sA_ln"); sA_tg = S("sA_tg"); sA_tev = S("sA_tev")
        sA_c = S("sA_c")

        with ExitStack() as ph:
            psb = lambda n, s, d: ph.enter_context(nc.sbuf_tensor(n, s, d))
            x_sb = [psb(f"xA{p}", [128, D], F32) for p in range(4)]
            ln_tm = [psb(f"lntmA{p}", [128, D], BF16) for p in range(4)]
            bnst = [psb(f"bnstA{p}", [128, 24], F32) for p in range(2)]
            stat = [psb(f"statA{p}", [128, 2], F32) for p in range(4)]
            veps = [psb(f"vepsA{p}", [128, 1], F32) for p in range(4)]
            stdv = [psb(f"stdA{p}", [128, 1], F32) for p in range(4)]
            rstd = [psb(f"rstdA{p}", [128, 1], F32) for p in range(4)]
            tr_ps = [ph.enter_context(nc.psum_tensor(f"trpsA{p}", [128, 512], F32))
                     for p in range(2)]

            with nc.Block() as blk:
                @blk.sync
                def _(sy):
                    sy.dma_start(masklT_sb[:, :], masklT_in[:, :]).then_inc(sA_c, 16)
                    sy.dma_start(ident_sb[:, :], ident_in[:, :]).then_inc(sA_c, 16)
                    for i in range(NT):
                        if i >= 4:
                            sy.wait_ge(sA_ln, i - 3)
                        sy.dma_start(x_sb[i % 4][:, :],
                                     x_in[i * 128:(i + 1) * 128, :])\
                            .then_inc(sA_x[i % 4], 16)

                @blk.vector
                def _(v):
                    for i in range(NT + 1):
                        if i < NT:
                            v.wait_ge(sA_x[i % 4], 16 * (i // 4 + 1))
                            for k in range(4):
                                ins = v.bn_stats(bnst[i % 2][:, 6 * k:6 * k + 6],
                                                 x_sb[i % 4][:, 512 * k:512 * k + 512])
                            vhop(v, ins)
                            vhop(v, v.bn_aggr(stat[i % 4][:, :], bnst[i % 2][:, :]))
                            v.tensor_scalar_add(veps[i % 4][:, 0:1],
                                                stat[i % 4][:, 1:2], EPS)\
                                .then_inc(sA_var, 1)
                        if i >= 1:
                            j = i - 1
                            v.wait_ge(sA_std, j + 1)
                            vhop(v, v.reciprocal(rstd[j % 4][:, 0:1],
                                                 stdv[j % 4][:, 0:1]))
                            if j >= 4:
                                v.wait_ge(sA_tg, 4 * (j - 3))
                            v.tensor_scalar(ln_tm[j % 4][:, :], x_sb[j % 4][:, :],
                                            stat[j % 4][:, 0:1], rstd[j % 4][:, 0:1],
                                            ALU.subtract, ALU.mult)\
                                .then_inc(sA_ln, 1)

                @blk.scalar
                def _(s):
                    for i in range(NT + 1):
                        if i < NT:
                            s.wait_ge(sA_var, i + 1)
                            if i >= 4:
                                s.wait_ge(sA_ln, i - 3)
                            s.activation(stdv[i % 4][:, 0:1], veps[i % 4][:, 0:1],
                                         AF.Sqrt).then_inc(sA_std, 1)
                        if i >= 1:
                            j = i - 1
                            for g in range(4):
                                gi = 4 * j + g
                                s.wait_ge(sA_tg, gi + 1)
                                for c in range(4):
                                    kc = 4 * g + c
                                    ins = s.activation(
                                        ln_fm[:, kc * (NT * 128) + j * 128:
                                              kc * (NT * 128) + (j + 1) * 128],
                                        tr_ps[gi % 2][:, c * 128:(c + 1) * 128],
                                        AF.Copy)
                                    if c == 3:
                                        ins.then_inc(sA_tev, 1)

                @blk.tensor
                def _(t):
                    t.wait_ge(sA_c, 32)
                    for j in range(NT):
                        t.wait_ge(sA_ln, j + 1)
                        for g in range(4):
                            gi = 4 * j + g
                            if gi >= 2:
                                t.wait_ge(sA_tev, gi - 1)
                            for c in range(4):
                                kc = 4 * g + c
                                ins = t.matmul(tr_ps[gi % 2][:, c * 128:(c + 1) * 128],
                                               ln_tm[j % 4][:, kc * 128:(kc + 1) * 128],
                                               ident_sb[:, :], start=True, stop=True)
                                if c == 3:
                                    ins.then_inc(sA_tg, 1)

                @blk.gpsimd
                def _(g):
                    g.wait_ge(sA_tev, 4 * NT)

        sB_wd = [S("sB_wd0"), S("sB_wd1")]
        sB_od = [S("sB_od0"), S("sB_od1")]
        sB_wc = S("sB_wc"); sB_mm = S("sB_mm")
        sB_ev = S("sB_ev"); sB_pe = S("sB_pe")

        with ExitStack() as ph:
            psb = lambda n, s, d: ph.enter_context(nc.sbuf_tensor(n, s, d))
            wb = [psb(f"wbB{p}", [128, 2048], BF16) for p in range(2)]
            ev = [psb(f"evB{p}", [128, 512], BF16) for p in range(2)]
            qkps = [ph.enter_context(nc.psum_tensor(f"qkpsB{p}", [128, 512], F32))
                    for p in range(2)]

            with nc.Block() as blk:
                @blk.gpsimd
                def _(g):
                    for it in range(32):
                        w, oc = divmod(it, 16)
                        src = (wqT if w == 0 else wkT)
                        if it >= 2:
                            g.wait_ge(sB_mm, 8 * (it - 1))
                        g.dma_start(
                            wb[it % 2][:, :].rearrange("p (kc h) -> p kc h", kc=16),
                            src[oc * 2048:(oc + 1) * 2048, :]
                            .rearrange("(kc p) h -> p kc h", p=128))\
                            .then_inc(sB_wd[it % 2], 16)

                @blk.tensor
                def _(t):
                    for it in range(32):
                        t.wait_ge(sB_wd[it % 2], 16 * (it // 2 + 1))
                        for gg in range(8):
                            j = it * 8 + gg
                            if j >= 2:
                                t.wait_ge(sB_ev, j - 1)
                            for kc in range(16):
                                ins = t.matmul(
                                    qkps[j % 2][:, :],
                                    wb[it % 2][:, kc * 128:(kc + 1) * 128],
                                    ln_fm[:, kc * (NT * 128) + gg * 512:
                                          kc * (NT * 128) + (gg + 1) * 512],
                                    start=(kc == 0), stop=(kc == 15))
                                if kc == 15:
                                    ins.then_inc(sB_mm, 1)

                @blk.vector
                def _(v):
                    for j in range(256):
                        mult = SCALE if j < 128 else 1.0
                        v.wait_ge(sB_mm, j + 1)
                        if j >= 2:
                            v.wait_ge(sB_od[j % 2], 16 * (j // 2))
                        v.tensor_scalar_mul(ev[j % 2][:, :],
                                            qkps[j % 2][:, :], mult)\
                            .then_inc(sB_ev, 1)

                @blk.scalar
                def _(s):
                    for j in range(256):
                        it, gg = divmod(j, 8)
                        w, oc = divmod(it, 16)
                        dst = (q_hbm if w == 0 else k_hbm)
                        s.wait_ge(sB_ev, j + 1)
                        s.dma_start(dst[:, oc * (NT * 128) + gg * 512:
                                        oc * (NT * 128) + (gg + 1) * 512],
                                    ev[j % 2][:, :]).then_inc(sB_od[j % 2], 16)
                    s.wait_ge(sB_od[0], 16 * 128)
                    s.wait_ge(sB_od[1], 16 * 128)

        sV_od = [S("sV_od0"), S("sV_od1")]
        sV_wd = [S("sV_wd0"), S("sV_wd1")]
        sV_wc = S("sV_wc"); sV_mm = S("sV_mm")
        sV_ev = S("sV_ev"); sV_pe = S("sV_pe")

        with ExitStack() as ph:
            psb = lambda n, s, d: ph.enter_context(nc.sbuf_tensor(n, s, d))
            vwb = [psb(f"vwbB{p}", [128, 8192], BF16) for p in range(2)]
            vev = [psb(f"vevB{p}", [128, 512], BF16) for p in range(2)]
            vps = [ph.enter_context(nc.psum_tensor(f"vpsB{p}", [128, 512], F32))
                   for p in range(2)]

            with nc.Block() as blk:
                @blk.gpsimd
                def _(g):
                    for ob in range(4):
                        if ob >= 2:
                            g.wait_ge(sV_mm, NT * (ob - 1))
                        g.dma_start(
                            vwb[ob % 2][:, :].rearrange("p (kc o) -> p kc o", kc=16),
                            wvT[ob * 2048:(ob + 1) * 2048, :]
                            .rearrange("(kc p) o -> p kc o", p=128))\
                            .then_inc(sV_wd[ob % 2], 16)

                @blk.tensor
                def _(t):
                    for ob in range(4):
                        t.wait_ge(sV_wd[ob % 2], 16 * (ob // 2 + 1))
                        for i in range(NT):
                            j = ob * NT + i
                            if j >= 2:
                                t.wait_ge(sV_ev, j - 1)
                            for kc in range(16):
                                ins = t.matmul(
                                    vps[j % 2][:, :],
                                    ln_fm[:, kc * (NT * 128) + i * 128:
                                          kc * (NT * 128) + (i + 1) * 128],
                                    vwb[ob % 2][:, kc * 512:(kc + 1) * 512],
                                    start=(kc == 0), stop=(kc == 15))
                                if kc == 15:
                                    ins.then_inc(sV_mm, 1)

                @blk.vector
                def _(v):
                    for j in range(4 * NT):
                        v.wait_ge(sV_mm, j + 1)
                        if j >= 2:
                            v.wait_ge(sV_od[j % 2], 64 * (j // 2))
                        v.tensor_copy(vev[j % 2][:, :], vps[j % 2][:, :])\
                            .then_inc(sV_ev, 1)

                @blk.scalar
                def _(s):
                    for j in range(4 * NT):
                        ob, i = divmod(j, NT)
                        sq, jt = divmod(i, NTS)
                        s.wait_ge(sV_ev, j + 1)
                        for hh in range(4):
                            h = ob * 4 + hh
                            r0 = ((sq * NH + h) * NTS + jt) * 128
                            s.dma_start(v_hbm[r0:r0 + 128, :],
                                        vev[j % 2][:, hh * 128:(hh + 1) * 128])\
                                .then_inc(sV_od[j % 2], 16)
                    s.wait_ge(sV_od[0], 64 * 64)
                    s.wait_ge(sV_od[1], 64 * 64)

        phab.close()

        sC_qkv = [S("sC_qkv0"), S("sC_qkv1")]
        sC_s = S("sC_s")
        sC_sevA = S("sC_sevA"); sC_sevB = S("sC_sevB")
        sC_negmx = S("sC_negmx"); sC_exp = S("sC_exp"); sC_pnorm = S("sC_pnorm")
        sC_tg = S("sC_tg"); sC_evA = S("sC_evA")
        sC_z = S("sC_z"); sC_zev = S("sC_zev")
        sC_zd = [S("sC_zd0"), S("sC_zd1")]
        sC_hdone = S("sC_hdone")
        sP_in = [S("sP_in0"), S("sP_in1")]
        sP_out = [S("sP_out0"), S("sP_out1")]

        def cntB(r):
            hh_full, i_part = divmod(r, 16)
            return hh_full * 8 + max(0, i_part - 8)

        grp_tab = []
        row_first_g = []
        absg = 0
        for hh in range(32):
            for i in range(16):
                row = []
                row_first_g.append(absg)
                for js in range(0, i + 1, 4):
                    row.append((absg, js, min(4, i + 1 - js)))
                    absg += 1
                grp_tab.append(row)
        NGRP = absg

        with ExitStack() as ph:
            psb = lambda n, s, d: ph.enter_context(nc.sbuf_tensor(n, s, d))
            q_sb = [psb(f"qC{p}", [128, 2048], BF16) for p in range(2)]
            k_sb = [psb(f"kC{p}", [128, 2048], BF16) for p in range(2)]
            v_sb = [psb(f"vC{p}", [128, 2048], BF16) for p in range(2)]
            scores = [psb(f"scC{p}", [128, 2048], F32) for p in range(2)]
            p_sb = [psb(f"pC{p}", [128, 2048], BF16) for p in range(2)]
            pn_sb = [psb(f"pnC{p}", [128, 2048], BF16) for p in range(4)]
            pt_sb = psb("ptC", [128, 16 * 512], BF16)
            zh_sb = [psb(f"zhC{p}", [128, 2048], BF16) for p in range(2)]
            mx = [psb(f"mxC{p}", [128, 1], F32) for p in range(2)]
            mxb = [psb(f"mxbC{p}", [128, 1], F32) for p in range(2)]
            negmx = [psb(f"ngC{p}", [128, 1], F32) for p in range(2)]
            sm = [psb(f"smC{p}", [128, 1], F32) for p in range(2)]
            rc = [psb(f"rcC{p}", [128, 1], F32) for p in range(2)]
            s_ps = ph.enter_context(nc.psum_tensor("spsC", [128, 2048], F32))
            pt_ps = [ph.enter_context(nc.psum_tensor(f"ptpsC{p}", [128, 512], F32))
                     for p in range(2)]
            z_ps = [ph.enter_context(nc.psum_tensor(f"zpsC{p}", [128, 512], F32))
                    for p in range(2)]

            ptv = pt_sb[:, :].rearrange("p (j c) -> p j c", j=16)
            ptsl = pt_sb[:, :].rearrange("p (j s c) -> p j s c", j=16, s=4)

            def qkv_load(sy, hh):
                sq, h = divmod(hh, 16)
                par = hh % 2
                base = h * (NT * 128) + sq * T
                sy.dma_start(q_sb[par][:, :], q_hbm[:, base:base + T])\
                    .then_inc(sC_qkv[par], 16)
                sy.dma_start(k_sb[par][:, :], k_hbm[:, base:base + T])\
                    .then_inc(sC_qkv[par], 16)
                r0 = (sq * NH + h) * NTS * 128
                sy.dma_start(
                    v_sb[par][:, :].rearrange("p (j hd) -> p j hd", j=16),
                    v_hbm[r0:r0 + NTS * 128, :]
                    .rearrange("(j p) hd -> p j hd", p=128))\
                    .then_inc(sC_qkv[par], 16)

            with nc.Block() as blk:
                @blk.sync
                def _(sy):
                    qkv_load(sy, 0)
                    qkv_load(sy, 1)
                    for hh in range(1, 31):
                        sy.wait_ge(sC_z, 4 * hh)
                        qkv_load(sy, hh + 1)
                    sy.wait_ge(sC_zd[0], 16 * 16)
                    sy.wait_ge(sC_zd[1], 16 * 16)

                @blk.tensor
                def _(t):
                    for hh in range(32):
                        par = hh % 2
                        for k in range(19):
                            if k <= 15:
                                i = k; r = hh * 16 + i
                                if k == 0:
                                    t.wait_ge(sC_qkv[hh % 2], 48 * (hh // 2 + 1))
                                t.wait_ge(sC_sevA, r)
                                if i >= 8:
                                    t.wait_ge(sC_sevB, cntB(r))
                                tot = (i + 1) * 128
                                nch = (tot + 511) // 512
                                dch = i // 4
                                for c in range(nch):
                                    wc = min(512, tot - c * 512)
                                    t.matmul(
                                        s_ps[:, c * 512:c * 512 + wc],
                                        q_sb[par][:, i * 128:(i + 1) * 128],
                                        k_sb[par][:, c * 512:c * 512 + wc],
                                        start=True, stop=(c != dch))
                                t.matmul(s_ps[:, i * 128:(i + 1) * 128],
                                         masklT_sb[:, :], ident_sb[:, :],
                                         start=False, stop=True)\
                                    .then_inc(sC_s, 1)
                            if k >= 3:
                                i2 = k - 3; r2 = hh * 16 + i2
                                blk = i2 // 4
                                t.wait_ge(sC_pnorm, r2 + 1)
                                for (g, js, jc) in grp_tab[r2]:
                                    if g >= 2:
                                        t.wait_ge(sC_evA, g - 1)
                                    for u in range(jc):
                                        jj = js + u
                                        ins = t.matmul(
                                            pt_ps[g % 2][:, u * 128:(u + 1) * 128],
                                            pn_sb[i2 % 4][:, jj * 128:(jj + 1) * 128],
                                            ident_sb[:, :], start=True, stop=True)
                                        if u == jc - 1:
                                            ins.then_inc(sC_tg, 1)
                                if i2 % 4 == 3:
                                    zb = hh * 4 + blk
                                    if zb >= 2:
                                        t.wait_ge(sC_zev, zb - 1)
                                    g_last = row_first_g[r2] + len(grp_tab[r2]) - 1
                                    t.wait_ge(sC_evA, g_last + 1)
                                    nj = 4 * blk + 4
                                    for jj in range(nj):
                                        lo = max(0, jj - 4 * blk)
                                        ins = t.matmul(
                                            z_ps[blk % 2][:, lo * 128:512],
                                            v_sb[par][:, jj * 128:(jj + 1) * 128],
                                            ptv[:, jj, lo * 128:512],
                                            start=(jj == 0), stop=(jj == nj - 1))
                                        if jj == nj - 1:
                                            ins.then_inc(sC_z, 1)


                @blk.vector
                def _(v):
                    for hh in range(32):
                        for k in range(19):
                            if k <= 15:
                                i = k; r = hh * 16 + i
                                v.wait_ge(sC_s, r + 1)
                                if r >= 2:
                                    v.wait_ge(sC_exp, r - 1)
                                tot = (i + 1) * 128
                                wA = min(tot, 1024)
                                v.tensor_scalar(scores[i % 2][:, 0:wA],
                                                s_ps[:, 0:wA], 0.0, None,
                                                ALU.add, ALU.max,
                                                accum_out=mx[i % 2][:, 0:1])\
                                    .then_inc(sC_sevA, 1)
                                v.wait_ge(sC_sevA, r + 1)
                                if tot > 1024:
                                    v.tensor_scalar(scores[i % 2][:, 1024:tot],
                                                    s_ps[:, 1024:tot], 0.0, None,
                                                    ALU.add, ALU.max,
                                                    accum_out=mxb[i % 2][:, 0:1])\
                                        .then_inc(sC_sevB, 1)
                                    v.wait_ge(sC_sevB, cntB(r) + 1)
                                    v.tensor_max(mx[i % 2][:, 0:1],
                                                 mx[i % 2][:, 0:1],
                                                 mxb[i % 2][:, 0:1])
                                v.tensor_scalar_mul(negmx[i % 2][:, 0:1],
                                                    mx[i % 2][:, 0:1], -1.0)\
                                    .then_inc(sC_negmx, 1)
                            if 1 <= k <= 16:
                                i1 = k - 1; r1 = hh * 16 + i1
                                v.wait_ge(sC_exp, r1 + 1)
                                vhop(v, v.reciprocal(rc[i1 % 2][:, 0:1],
                                                     sm[i1 % 2][:, 0:1]))
                                if r1 >= 4:
                                    g_end = row_first_g[r1 - 4] + len(grp_tab[r1 - 4])
                                    v.wait_ge(sC_tg, g_end)
                                tot = (i1 + 1) * 128
                                v.tensor_scalar(pn_sb[i1 % 4][:, 0:tot],
                                                p_sb[i1 % 2][:, 0:tot],
                                                rc[i1 % 2][:, 0:1], None, ALU.mult)\
                                    .then_inc(sC_pnorm, 1)

                @blk.scalar
                def _(s):
                    for hh in range(32):
                        for k in range(19):
                            if 1 <= k <= 16:
                                i1 = k - 1; r1 = hh * 16 + i1
                                s.wait_ge(sC_negmx, r1 + 1)
                                if r1 >= 2:
                                    s.wait_ge(sC_pnorm, r1 - 1)
                                tot = (i1 + 1) * 128
                                s.activation(p_sb[i1 % 2][:, 0:tot],
                                             scores[i1 % 2][:, 0:tot], AF.Exp,
                                             bias=negmx[i1 % 2][:, 0:1],
                                             accum_out=sm[i1 % 2][:, 0:1])\
                                    .then_inc(sC_exp, 1)
                            if k >= 3:
                                i2 = k - 3; r2 = hh * 16 + i2
                                blk = i2 // 4
                                srow = i2 % 4
                                first = True
                                for (g, js, jc) in grp_tab[r2]:
                                    s.wait_ge(sC_tg, g + 1)
                                    if first:
                                        s.wait_ge(sC_z, hh * 4 + blk)
                                        first = False
                                    s.activation(
                                        ptsl[:, js:js + jc, srow, :],
                                        pt_ps[g % 2][:, 0:jc * 128]
                                        .rearrange("p (j c) -> p j c", j=jc),
                                        AF.Copy).then_inc(sC_evA, 1)
                                if srow == 3:
                                    zb = hh * 4 + blk
                                    s.wait_ge(sC_z, zb + 1)
                                    if blk == 0 and hh >= 2:
                                        s.wait_ge(sC_zd[hh % 2], 16 * (hh // 2))
                                    s.activation(
                                        zh_sb[hh % 2][:, blk * 512:(blk + 1) * 512],
                                        z_ps[blk % 2][:, :], AF.Copy)\
                                        .then_inc(sC_zev, 1)
                        sq, h = divmod(hh, 16)
                        s.wait_ge(sC_zev, 4 * (hh + 1))
                        s.dma_start(z_hbm[:, h * (NT * 128) + sq * T:
                                          h * (NT * 128) + sq * T + T],
                                    zh_sb[hh % 2][:, :]).then_inc(sC_zd[hh % 2], 16)
                    s.wait_ge(sC_zd[0], 16 * 16)
                    s.wait_ge(sC_zd[1], 16 * 16)

                @blk.gpsimd
                def _(g):
                    np_ = 0
                    for b in range(64):
                        g.dma_start(wg_bf[b * 128:(b + 1) * 128, :]
                                    .rearrange("p (kc h) -> p kc h", kc=16),
                                    wgT[b * 2048:(b + 1) * 2048, :]
                                    .rearrange("(kc p) h -> p kc h", p=128))\
                            .then_inc(sP_out[0], 16)
                        g.dma_start(wu_bf[b * 128:(b + 1) * 128, :]
                                    .rearrange("p (kc h) -> p kc h", kc=16),
                                    wuT[b * 2048:(b + 1) * 2048, :]
                                    .rearrange("(kc p) h -> p kc h", p=128))\
                            .then_inc(sP_out[0], 16)
                        g.dma_start(wd_bf[b * 128:(b + 1) * 128, :],
                                    wdT[b * 128:(b + 1) * 128, :])\
                            .then_inc(sP_out[0], 16)
                        np_ += 3
                    g.wait_ge(sP_out[0], 16 * np_)

        sW_zd = [S("sW_zd0"), S("sW_zd1"), S("sW_zd2")]
        sW_xd = [S("sW_xd0"), S("sW_xd1")]
        sW_x2d = [S("sW_x2d0"), S("sW_x2d1")]
        sW_wd = S("sW_wd"); sW_wc = S("sW_wc")
        sW_mm = S("sW_mm"); sW_mmg = S("sW_mmg")
        sW_ev = S("sW_ev"); sW_pe = S("sW_pe")

        with ExitStack() as ph:
            psb = lambda n, s, d: ph.enter_context(nc.sbuf_tensor(n, s, d))
            c2wb = [psb(f"c2wb{p}", [128, 8192], BF16) for p in range(2)]
            zg_sb = [psb(f"zgC2{p}", [128, 8192], BF16) for p in range(3)]
            xo_sb = [psb(f"xoC2{p}", [128, 512], F32) for p in range(2)]
            x2o = [psb(f"x2oC2{p}", [128, 512], F32) for p in range(2)]
            c2ps = [ph.enter_context(nc.psum_tensor(f"c2ps{p}", [128, 512], F32))
                    for p in range(2)]

            with nc.Block() as blk:
                @blk.gpsimd
                def _(g):
                    for ob in range(4):
                        if ob >= 2:
                            g.wait_ge(sW_mm, 32 * (ob - 1))
                        g.dma_start(
                            c2wb[ob % 2][:, :].rearrange("p (dc o) -> p dc o", dc=16),
                            woT[ob * 2048:(ob + 1) * 2048, :]
                            .rearrange("(dc p) o -> p dc o", p=128))\
                            .then_inc(sW_wd, 16)

                @blk.sync
                def _(sy):
                    for ob in range(4):
                        for grp in range(NG):
                            g_it = ob * NG + grp
                            if g_it >= 3:
                                sy.wait_ge(sW_mm, 4 * (g_it - 2))
                            sy.dma_start(
                                zg_sb[g_it % 3][:, :]
                                .rearrange("p (dc c) -> p dc c", dc=16),
                                z_hbm[:, :]
                                .rearrange("p (dc t) -> p dc t", dc=16)
                                [:, :, grp * 512:(grp + 1) * 512])\
                                .then_inc(sW_zd[g_it % 3], 16)
                            for tc in range(4):
                                it = g_it * 4 + tc
                                tg = grp * 4 + tc
                                if it >= 2:
                                    sy.wait_ge(sW_ev, it - 1)
                                sy.dma_start(
                                    xo_sb[it % 2][:, :],
                                    x_in[tg * 128:(tg + 1) * 128,
                                         ob * 512:(ob + 1) * 512])\
                                    .then_inc(sW_xd[it % 2], 16)

                @blk.tensor
                def _(t):
                    for ob in range(4):
                        t.wait_ge(sW_wd, 16 * (ob + 1))
                        for grp in range(NG):
                            g_it = ob * NG + grp
                            t.wait_ge(sW_zd[g_it % 3], 16 * (g_it // 3 + 1))
                            for tc in range(4):
                                it = g_it * 4 + tc
                                if it >= 2:
                                    t.wait_ge(sW_ev, it - 1)
                                for dc in range(16):
                                    ins = t.matmul(
                                        c2ps[it % 2][:, :],
                                        zg_sb[g_it % 3][:, dc * 512 + tc * 128:
                                                        dc * 512 + (tc + 1) * 128],
                                        c2wb[ob % 2][:, dc * 512:(dc + 1) * 512],
                                        start=(dc == 0), stop=(dc == 15))
                                    if dc == 15:
                                        ins.then_inc(sW_mm, 1)

                @blk.vector
                def _(v):
                    for it in range(128):
                        v.wait_ge(sW_mm, it + 1)
                        v.wait_ge(sW_xd[it % 2], 16 * (it // 2 + 1))
                        if it >= 2:
                            v.wait_ge(sW_x2d[it % 2], 16 * (it // 2))
                        v.tensor_tensor(x2o[it % 2][:, :], c2ps[it % 2][:, :],
                                        xo_sb[it % 2][:, :], ALU.add)\
                            .then_inc(sW_ev, 1)

                @blk.scalar
                def _(s):
                    for it in range(128):
                        g_it, tc = divmod(it, 4)
                        ob, grp = divmod(g_it, NG)
                        tg = grp * 4 + tc
                        s.wait_ge(sW_ev, it + 1)
                        s.dma_start(x2_hbm[tg * 128:(tg + 1) * 128,
                                           ob * 512:(ob + 1) * 512],
                                    x2o[it % 2][:, :]).then_inc(sW_x2d[it % 2], 16)
                    s.wait_ge(sW_x2d[0], 16 * 64)
                    s.wait_ge(sW_x2d[1], 16 * 64)

        sD_x2 = [S(f"sD_x2{p}") for p in range(4)]
        sD_wgd = [S("sD_wgd0"), S("sD_wgd1")]
        sD_dwd = [S(f"sD_dwd{p}") for p in range(4)]
        sD_od = [S(f"sD_od{p}") for p in range(4)]
        sD_var = S("sD_var"); sD_std = S("sD_std")
        sD_ln = S("sD_ln"); sD_tg = S("sD_tg"); sD_tev = S("sD_tev")
        sD_gu = S("sD_gu")
        sD_g = S("sD_g"); sD_u = S("sD_u"); sD_si = S("sD_si"); sD_mu = S("sD_mu")
        sD_dcons = S("sD_dcons"); sD_d = S("sD_d")
        sD_ev = S("sD_ev"); sD_dbg = S("sD_dbg")

        sD_xo = [S(f"sD_xo{p}") for p in range(4)]

        with ExitStack() as ph:
            psb = lambda n, s, d: ph.enter_context(nc.sbuf_tensor(n, s, d))
            x2t = [psb(f"x2tD{p}", [128, D], F32) for p in range(4)]
            xo_d = [psb(f"xoD{p}", [128, 512], F32) for p in range(4)]
            ln2fm = psb("ln2fmD", [128, 16 * 512], BF16)
            gated = psb("gatedD", [128, 64 * 512], BF16)
            wgb = [psb(f"wgbD{p}", [128, 2048], BF16) for p in range(2)]
            wub = [psb(f"wubD{p}", [128, 2048], BF16) for p in range(2)]
            wdb = [psb(f"wdbD{p}", [128, 2048], BF16) for p in range(4)]
            silu_sb = [psb(f"siluD{p}", [128, 512], F32) for p in range(2)]
            outo = [psb(f"outoD{p}", [128, 512], F32) for p in range(4)]
            ln2tm = [psb(f"ln2tmD{p}", [128, D], BF16) for p in range(4)]
            bnst2 = [psb(f"bnstD{p}", [128, 24], F32) for p in range(2)]
            stat2 = [psb(f"statD{p}", [128, 2], F32) for p in range(4)]
            veps2 = [psb(f"vepsD{p}", [128, 1], F32) for p in range(4)]
            stdv2 = [psb(f"stdD{p}", [128, 1], F32) for p in range(4)]
            rstd2 = [psb(f"rstdD{p}", [128, 1], F32) for p in range(4)]
            g_ps = ph.enter_context(nc.psum_tensor("gpsD", [128, 512], F32))
            u_ps = ph.enter_context(nc.psum_tensor("upsD", [128, 512], F32))
            d_ps = [ph.enter_context(nc.psum_tensor(f"dpsD{p}", [128, 512], F32))
                    for p in range(4)]
            tr2_ps = [ph.enter_context(nc.psum_tensor(f"tr2D{p}", [128, 512], F32))
                      for p in range(2)]

            with nc.Block() as blk:
                @blk.sync
                def _(sy):
                    def x2t_loads(gx, rng=range(4)):
                        for tt in rng:
                            ti = gx * 4 + tt
                            if ti >= 4:
                                sy.wait_ge(sD_ln, ti - 3)
                            sy.dma_start(
                                x2t[ti % 4][:, :],
                                x2_hbm[ti * 128:(ti + 1) * 128, :])\
                                .then_inc(sD_x2[ti % 4], 16)

                    for g in range(NG):
                        if g == 0:
                            x2t_loads(0)
                            x2t_loads(1, range(3))
                        elif g + 1 < NG:
                            x2t_loads(g + 1, range(3))
                        def dcons_cnt(upto):
                            return upto - upto // 16

                        for ob in range(4):
                            for m in range(16):
                                st = (g * 4 + ob) * 16 + m
                                if st >= 4:
                                    if (st - 4) % 16 == 15:
                                        sy.wait_ge(sD_d, (st - 4) // 16 + 1)
                                    else:
                                        sy.wait_ge(sD_dcons, dcons_cnt(st - 3))
                                sy.dma_start(
                                    wdb[st % 4][:, :]
                                    .rearrange("p (c o) -> p c o", c=4),
                                    wd_bf[m * 512:(m + 1) * 512,
                                          ob * 512:(ob + 1) * 512]
                                    .rearrange("(c p) o -> p c o", p=128))\
                                    .then_inc(sD_dwd[st % 4], 16)
                            if ob == 0 and g + 1 < NG:
                                x2t_loads(g + 1, range(3, 4))
                            for tc in range(4):
                                oidx = (g * 4 + ob) * 4 + tc
                                tg = g * 4 + tc
                                if oidx >= 4:
                                    sy.wait_ge(sD_ev, oidx - 3)
                                sy.dma_start(
                                    xo_d[oidx % 4][:, :],
                                    x2_hbm[tg * 128:(tg + 1) * 128,
                                           ob * 512:(ob + 1) * 512])\
                                    .then_inc(sD_xo[oidx % 4], 16)

                @blk.gpsimd
                def _(g):
                    for gg in range(NG):
                        if debug and gg == 1:
                            g.wait_ge(sD_d, 4)
                            g.dma_start(dbg_ln2fm[:, :], ln2fm[:, :])\
                                .then_inc(sD_dbg, 16)
                            g.dma_start(dbg_gated[:, :], gated[:, :])\
                                .then_inc(sD_dbg, 16)
                            g.wait_ge(sD_dbg, 32)
                        for hh in range(64):
                            ghh = gg * 64 + hh
                            if ghh >= 2:
                                g.wait_ge(sD_u, ghh - 1)
                            g.dma_start(wgb[ghh % 2][:, :],
                                        wg_bf[hh * 128:(hh + 1) * 128, :])\
                                .then_inc(sD_wgd[ghh % 2], 16)
                            g.dma_start(wub[ghh % 2][:, :],
                                        wu_bf[hh * 128:(hh + 1) * 128, :])\
                                .then_inc(sD_wgd[ghh % 2], 16)

                @blk.vector
                def _(v):
                    def ln2_all(gx):
                        for tt in range(4):
                            ti = gx * 4 + tt
                            v.wait_ge(sD_x2[ti % 4], 16 * (ti // 4 + 1))
                            for q in range(4):
                                ins = v.bn_stats(
                                    bnst2[tt % 2][:, 6 * q:6 * q + 6],
                                    x2t[ti % 4][:, 512 * q:512 * (q + 1)])
                            vhop(v, ins)
                            vhop(v, v.bn_aggr(stat2[tt][:, :],
                                              bnst2[tt % 2][:, :]))
                            v.tensor_scalar_add(veps2[tt][:, 0:1],
                                                stat2[tt][:, 1:2], EPS)\
                                .then_inc(sD_var, 1)
                        for tt in range(4):
                            ti = gx * 4 + tt
                            v.wait_ge(sD_std, ti + 1)
                            vhop(v, v.reciprocal(rstd2[tt][:, 0:1],
                                                 stdv2[tt][:, 0:1]))
                            if ti >= 4:
                                v.wait_ge(sD_tg, 4 * (ti - 3))
                            v.tensor_scalar(ln2tm[tt][:, :],
                                            x2t[ti % 4][:, :],
                                            stat2[tt][:, 0:1],
                                            rstd2[tt][:, 0:1],
                                            ALU.subtract, ALU.mult)\
                                .then_inc(sD_ln, 1)

                    ln2_all(0)
                    for g in range(NG):
                        for hh in range(64):
                            ghh = g * 64 + hh
                            v.wait_ge(sD_si, ghh + 1)
                            v.wait_ge(sD_u, ghh + 1)
                            v.tensor_tensor(gated[:, hh * 512:(hh + 1) * 512],
                                            silu_sb[ghh % 2][:, :], u_ps[:, :],
                                            ALU.mult).then_inc(sD_mu, 1)
                        if g + 1 < NG:
                            ln2_all(g + 1)
                        for ob in range(4):
                            v.wait_ge(sD_d, g * 4 + ob + 1)
                            for tc in range(4):
                                oidx = (g * 4 + ob) * 4 + tc
                                if oidx >= 4:
                                    v.wait_ge(sD_od[oidx % 4], 16 * (oidx // 4))
                                v.wait_ge(sD_xo[oidx % 4], 16 * (oidx // 4 + 1))
                                v.tensor_tensor(
                                    outo[oidx % 4][:, :], d_ps[tc][:, :],
                                    xo_d[oidx % 4][:, :],
                                    ALU.add).then_inc(sD_ev, 1)

                @blk.scalar
                def _(s):
                    def sqrt4(gx):
                        for tt in range(4):
                            ti = gx * 4 + tt
                            s.wait_ge(sD_var, ti + 1)
                            if ti >= 4:
                                s.wait_ge(sD_ln, ti - 3)
                            s.activation(stdv2[tt][:, 0:1],
                                         veps2[tt][:, 0:1], AF.Sqrt)\
                                .then_inc(sD_std, 1)

                    def tevacs(gx):
                        for tt in range(4):
                            for gr in range(4):
                                gi = 4 * (gx * 4 + tt) + gr
                                s.wait_ge(sD_tg, gi + 1)
                                for c in range(4):
                                    kc = 4 * gr + c
                                    ins = s.activation(
                                        ln2fm[:, kc * 512 + tt * 128:
                                              kc * 512 + (tt + 1) * 128],
                                        tr2_ps[gi % 2][:, c * 128:(c + 1) * 128],
                                        AF.Copy)
                                    if c == 3:
                                        ins.then_inc(sD_tev, 1)

                    sqrt4(0)
                    tevacs(0)
                    for g in range(NG):
                        for hh in range(64):
                            ghh = g * 64 + hh
                            s.wait_ge(sD_g, ghh + 1)
                            if ghh >= 2:
                                s.wait_ge(sD_mu, ghh - 1)
                            s.activation(silu_sb[ghh % 2][:, :], g_ps[:, :],
                                         AF.Silu).then_inc(sD_si, 1)
                        if g + 1 < NG:
                            sqrt4(g + 1)
                        for ob in range(4):
                            for tc in range(4):
                                oidx = (g * 4 + ob) * 4 + tc
                                tg = g * 4 + tc
                                s.wait_ge(sD_ev, oidx + 1)
                                s.dma_start(out[tg * 128:(tg + 1) * 128,
                                                ob * 512:(ob + 1) * 512],
                                            outo[oidx % 4][:, :])\
                                    .then_inc(sD_od[oidx % 4], 16)
                        if g + 1 < NG:
                            tevacs(g + 1)
                    for pp in range(4):
                        s.wait_ge(sD_od[pp], 16 * 32)

                @blk.tensor
                def _(t):
                    def tgroups(gx):
                        for tt in range(4):
                            ti = gx * 4 + tt
                            t.wait_ge(sD_ln, ti + 1)
                            for gr in range(4):
                                gi = 4 * ti + gr
                                if gi >= 2:
                                    t.wait_ge(sD_tev, gi - 1)
                                for c in range(4):
                                    kc = 4 * gr + c
                                    ins = t.matmul(
                                        tr2_ps[gi % 2][:, c * 128:(c + 1) * 128],
                                        ln2tm[tt][:, kc * 128:(kc + 1) * 128],
                                        ident_sb[:, :], start=True, stop=True)
                                    if c == 3:
                                        ins.then_inc(sD_tg, 1)

                    tgroups(0)
                    for g in range(NG):
                        if debug and g == 1:
                            t.wait_ge(sD_dbg, 32)
                        t.wait_ge(sD_tev, 16 * (g + 1))
                        for hh in range(64):
                            ghh = g * 64 + hh
                            t.wait_ge(sD_wgd[ghh % 2], 32 * (ghh // 2 + 1))
                            t.wait_ge(sD_si, ghh)
                            for kc in range(16):
                                ins = t.matmul(g_ps[:, :],
                                               wgb[ghh % 2][:, kc * 128:(kc + 1) * 128],
                                               ln2fm[:, kc * 512:(kc + 1) * 512],
                                               start=(kc == 0), stop=(kc == 15))
                                if kc == 15:
                                    ins.then_inc(sD_g, 1)
                            t.wait_ge(sD_mu, ghh)
                            for kc in range(16):
                                ins = t.matmul(u_ps[:, :],
                                               wub[ghh % 2][:, kc * 128:(kc + 1) * 128],
                                               ln2fm[:, kc * 512:(kc + 1) * 512],
                                               start=(kc == 0), stop=(kc == 15))
                                if kc == 15:
                                    ins.then_inc(sD_u, 1)
                        for ob in range(4):
                            t.wait_ge(sD_ev, 4 * (g * 4 + ob))
                            for m in range(16):
                                st = (g * 4 + ob) * 16 + m
                                t.wait_ge(sD_dwd[st % 4], 16 * (st // 4 + 1))
                                t.wait_ge(sD_mu, g * 64 + 4 * m + 4)
                                wdb_st = wdb[st % 4]
                                for c in range(4):
                                    hh = 4 * m + c
                                    for tc in range(4):
                                        ins = t.matmul(
                                            d_ps[tc][:, :],
                                            gated[:, hh * 512 + tc * 128:
                                                  hh * 512 + (tc + 1) * 128],
                                            wdb_st[:, c * 512:(c + 1) * 512],
                                            start=(hh == 0), stop=(hh == 63))
                                        if hh == 63 and tc == 3:
                                            ins.then_inc(sD_d, 1)
                                        elif c == 3 and tc == 3:
                                            ins.then_inc(sD_dcons, 1)
                        if g + 1 < NG:
                            tgroups(g + 1)

    nc.finalize()
    return nc


_NC_CACHE = {}


def _get_nc(debug=False):
    key = bool(debug)
    if key not in _NC_CACHE:
        _NC_CACHE[key] = build_nc(debug=key)
    return _NC_CACHE[key]


def _prep_weights(Wq, Wk, Wv, Wo, w_gate, w_up, w_down):
    f32 = np.float32
    wq = np.ascontiguousarray(np.asarray(Wq, dtype=f32).T.reshape(2048, 16, 128)
                              .transpose(1, 0, 2)).reshape(16 * 2048, 128)
    wk = np.ascontiguousarray(np.asarray(Wk, dtype=f32).T.reshape(2048, 16, 128)
                              .transpose(1, 0, 2)).reshape(16 * 2048, 128)
    wv = np.ascontiguousarray(np.asarray(Wv, dtype=f32).T.reshape(2048, 4, 512)
                              .transpose(1, 0, 2)).reshape(4 * 2048, 512)
    wo = np.ascontiguousarray(np.asarray(Wo, dtype=f32).T.reshape(2048, 4, 512)
                              .transpose(1, 0, 2)).reshape(4 * 2048, 512)
    wg = np.ascontiguousarray(np.asarray(w_gate, dtype=f32).T.reshape(2048, 64, 128)
                              .transpose(1, 0, 2)).reshape(64 * 2048, 128)
    wu = np.ascontiguousarray(np.asarray(w_up, dtype=f32).T.reshape(2048, 64, 128)
                              .transpose(1, 0, 2)).reshape(64 * 2048, 128)
    wd = np.ascontiguousarray(np.asarray(w_down, dtype=f32).T)
    return wq, wk, wv, wo, wg, wu, wd


def kernel(L0_attn_x, L0_attn_Wq, L0_attn_Wk, L0_attn_Wv, L0_attn_Wo,
           L0_ffn_w_gate, L0_ffn_w_up, L0_ffn_w_down):
    x = np.ascontiguousarray(np.asarray(L0_attn_x, dtype=np.float32))
    wq, wk, wv, wo, wg, wu, wd = _prep_weights(
        L0_attn_Wq, L0_attn_Wk, L0_attn_Wv, L0_attn_Wo,
        L0_ffn_w_gate, L0_ffn_w_up, L0_ffn_w_down)
    masklT = np.tril(np.full((128, 128), -1e9, dtype=np.float32), k=-1)\
        .astype(ml_dtypes.bfloat16)
    ident = np.eye(128, dtype=ml_dtypes.bfloat16)

    nc = _get_nc(debug=False)
    in_maps = []
    for c in range(NCORES):
        in_maps.append({
            "x_in": x[2 * c:2 * c + 2].reshape(NSEQ * T, D),
            "wqT": wq, "wkT": wk, "wvT": wv, "woT": wo,
            "wgT": wg, "wuT": wu, "wdT": wd,
            "masklT_in": masklT, "ident_in": ident,
        })
    res = run_bass_kernel_spmd(nc, in_maps, core_ids=list(range(NCORES)))
    outs = [r["out"].reshape(NSEQ, T, D) for r in res.results]
    return np.concatenate(outs, axis=0)


# version 13
# speedup vs baseline: 1.2004x; 1.0784x over previous
import numpy as np
import ml_dtypes
from contextlib import ExitStack

import concourse.bass as bass
import concourse.mybir as mybir
from concourse.bass_utils import run_bass_kernel_spmd

F32 = mybir.dt.float32
BF16 = mybir.dt.bfloat16
AF = mybir.ActivationFunctionType
ALU = mybir.AluOpType
AX = mybir.AxisListType

NCORES = 8
NSEQ = 2
T = 2048
D = 2048
NH = 16
HD = 128
HID = 8192
NT = NSEQ * T // 128
NTS = T // 128
SCALE = 1.0 / float(np.sqrt(HD))
EPS = 1e-5
GT = 512
NG = NSEQ * T // GT


class Sem:
    def __init__(self, ctx, nc, name):
        self.h = ctx.enter_context(nc.semaphore(name))


def build_nc(debug=False):
    nc = bass.Bass("TRN2", target_bir_lowering=False)

    x_in = nc.dram_tensor("x_in", [NSEQ * T, D], F32, kind="ExternalInput")
    wqT = nc.dram_tensor("wqT", [16 * 2048, 128], F32, kind="ExternalInput")
    wkT = nc.dram_tensor("wkT", [16 * 2048, 128], F32, kind="ExternalInput")
    wvT = nc.dram_tensor("wvT", [4 * 2048, 512], F32, kind="ExternalInput")
    woT = nc.dram_tensor("woT", [4 * 2048, 512], F32, kind="ExternalInput")
    wgT = nc.dram_tensor("wgT", [64 * 2048, 128], F32, kind="ExternalInput")
    wuT = nc.dram_tensor("wuT", [64 * 2048, 128], F32, kind="ExternalInput")
    wdT = nc.dram_tensor("wdT", [8192, 2048], F32, kind="ExternalInput")
    masklT_in = nc.dram_tensor("masklT_in", [128, 128], BF16, kind="ExternalInput")
    ident_in = nc.dram_tensor("ident_in", [128, 128], BF16, kind="ExternalInput")

    skind = "ExternalOutput" if debug else "Internal"
    q_hbm = nc.dram_tensor("q_hbm", [128, NH * NSEQ * T], BF16, kind=skind)
    k_hbm = nc.dram_tensor("k_hbm", [128, NH * NSEQ * T], BF16, kind=skind)
    v_hbm = nc.dram_tensor("v_hbm", [NSEQ * NH * NTS * 128, 128], BF16, kind=skind)
    z_hbm = nc.dram_tensor("z_hbm", [128, NH * NSEQ * T], BF16, kind=skind)
    x2_hbm = nc.dram_tensor("x2_hbm", [NSEQ * T, D], F32, kind=skind)
    wg_bf = nc.dram_tensor("wg_bf", [64 * 128, 2048], BF16, kind=skind)
    wu_bf = nc.dram_tensor("wu_bf", [64 * 128, 2048], BF16, kind=skind)
    wd_bf = nc.dram_tensor("wd_bf", [8192, 2048], BF16, kind=skind)
    out = nc.dram_tensor("out", [NSEQ * T, D], F32, kind="ExternalOutput")
    if debug:
        dbg_ln2fm = nc.dram_tensor("dbg_ln2fm", [128, 16 * 512], BF16,
                                   kind="ExternalOutput")
        dbg_gated = nc.dram_tensor("dbg_gated", [128, 64 * 512], BF16,
                                   kind="ExternalOutput")

    with ExitStack() as top:
        sb = lambda n, s, d: top.enter_context(nc.sbuf_tensor(n, s, d))
        S = lambda n: Sem(top, nc, n).h

        sVself = S("sVself")
        vself_n = [0]

        def vhop(v, ins):
            vself_n[0] += 1
            ins.then_inc(sVself, 1)
            v.wait_ge(sVself, vself_n[0])

        masklT_sb = sb("masklT_sb", [128, 128], BF16)
        ident_sb = sb("ident_sb", [128, 128], BF16)

        phab = top.enter_context(ExitStack())
        ln_fm = phab.enter_context(nc.sbuf_tensor("ln_fm", [128, 16 * NT * 128], BF16))

        sA_x = [S(f"sA_x{p}") for p in range(4)]
        sA_var = S("sA_var"); sA_std = S("sA_std")
        sA_ln = S("sA_ln"); sA_tg = S("sA_tg"); sA_tev = S("sA_tev")
        sA_c = S("sA_c")

        with ExitStack() as ph:
            psb = lambda n, s, d: ph.enter_context(nc.sbuf_tensor(n, s, d))
            x_sb = [psb(f"xA{p}", [128, D], F32) for p in range(4)]
            ln_tm = [psb(f"lntmA{p}", [128, D], BF16) for p in range(4)]
            bnst = [psb(f"bnstA{p}", [128, 24], F32) for p in range(2)]
            stat = [psb(f"statA{p}", [128, 2], F32) for p in range(4)]
            veps = [psb(f"vepsA{p}", [128, 1], F32) for p in range(4)]
            stdv = [psb(f"stdA{p}", [128, 1], F32) for p in range(4)]
            rstd = [psb(f"rstdA{p}", [128, 1], F32) for p in range(4)]
            tr_ps = [ph.enter_context(nc.psum_tensor(f"trpsA{p}", [128, 512], F32))
                     for p in range(2)]

            with nc.Block() as blk:
                @blk.sync
                def _(sy):
                    sy.dma_start(masklT_sb[:, :], masklT_in[:, :]).then_inc(sA_c, 16)
                    sy.dma_start(ident_sb[:, :], ident_in[:, :]).then_inc(sA_c, 16)
                    for i in range(NT):
                        if i >= 4:
                            sy.wait_ge(sA_ln, i - 3)
                        sy.dma_start(x_sb[i % 4][:, :],
                                     x_in[i * 128:(i + 1) * 128, :])\
                            .then_inc(sA_x[i % 4], 16)

                @blk.vector
                def _(v):
                    for i in range(NT + 1):
                        if i < NT:
                            v.wait_ge(sA_x[i % 4], 16 * (i // 4 + 1))
                            for k in range(4):
                                ins = v.bn_stats(bnst[i % 2][:, 6 * k:6 * k + 6],
                                                 x_sb[i % 4][:, 512 * k:512 * k + 512])
                            vhop(v, ins)
                            vhop(v, v.bn_aggr(stat[i % 4][:, :], bnst[i % 2][:, :]))
                            v.tensor_scalar_add(veps[i % 4][:, 0:1],
                                                stat[i % 4][:, 1:2], EPS)\
                                .then_inc(sA_var, 1)
                        if i >= 1:
                            j = i - 1
                            v.wait_ge(sA_std, j + 1)
                            vhop(v, v.reciprocal(rstd[j % 4][:, 0:1],
                                                 stdv[j % 4][:, 0:1]))
                            if j >= 4:
                                v.wait_ge(sA_tg, 4 * (j - 3))
                            v.tensor_scalar(ln_tm[j % 4][:, :], x_sb[j % 4][:, :],
                                            stat[j % 4][:, 0:1], rstd[j % 4][:, 0:1],
                                            ALU.subtract, ALU.mult)\
                                .then_inc(sA_ln, 1)

                @blk.scalar
                def _(s):
                    for i in range(NT + 1):
                        if i < NT:
                            s.wait_ge(sA_var, i + 1)
                            if i >= 4:
                                s.wait_ge(sA_ln, i - 3)
                            s.activation(stdv[i % 4][:, 0:1], veps[i % 4][:, 0:1],
                                         AF.Sqrt).then_inc(sA_std, 1)
                        if i >= 1:
                            j = i - 1
                            for g in range(4):
                                gi = 4 * j + g
                                s.wait_ge(sA_tg, gi + 1)
                                for c in range(4):
                                    kc = 4 * g + c
                                    ins = s.activation(
                                        ln_fm[:, kc * (NT * 128) + j * 128:
                                              kc * (NT * 128) + (j + 1) * 128],
                                        tr_ps[gi % 2][:, c * 128:(c + 1) * 128],
                                        AF.Copy)
                                    if c == 3:
                                        ins.then_inc(sA_tev, 1)

                @blk.tensor
                def _(t):
                    t.wait_ge(sA_c, 32)
                    for j in range(NT):
                        t.wait_ge(sA_ln, j + 1)
                        for g in range(4):
                            gi = 4 * j + g
                            if gi >= 2:
                                t.wait_ge(sA_tev, gi - 1)
                            for c in range(4):
                                kc = 4 * g + c
                                ins = t.matmul(tr_ps[gi % 2][:, c * 128:(c + 1) * 128],
                                               ln_tm[j % 4][:, kc * 128:(kc + 1) * 128],
                                               ident_sb[:, :], start=True, stop=True)
                                if c == 3:
                                    ins.then_inc(sA_tg, 1)

                @blk.gpsimd
                def _(g):
                    g.wait_ge(sA_tev, 4 * NT)

        sB_wd = [S("sB_wd0"), S("sB_wd1")]
        sB_od = [S("sB_od0"), S("sB_od1")]
        sB_wc = S("sB_wc"); sB_mm = S("sB_mm")
        sB_ev = S("sB_ev"); sB_pe = S("sB_pe")

        with ExitStack() as ph:
            psb = lambda n, s, d: ph.enter_context(nc.sbuf_tensor(n, s, d))
            wb = [psb(f"wbB{p}", [128, 2048], BF16) for p in range(2)]
            ev = [psb(f"evB{p}", [128, 512], BF16) for p in range(2)]
            qkps = [ph.enter_context(nc.psum_tensor(f"qkpsB{p}", [128, 512], F32))
                    for p in range(2)]

            with nc.Block() as blk:
                @blk.gpsimd
                def _(g):
                    for it in range(32):
                        w, oc = divmod(it, 16)
                        src = (wqT if w == 0 else wkT)
                        if it >= 2:
                            g.wait_ge(sB_mm, 8 * (it - 1))
                        g.dma_start(
                            wb[it % 2][:, :].rearrange("p (kc h) -> p kc h", kc=16),
                            src[oc * 2048:(oc + 1) * 2048, :]
                            .rearrange("(kc p) h -> p kc h", p=128))\
                            .then_inc(sB_wd[it % 2], 16)

                @blk.tensor
                def _(t):
                    for it in range(32):
                        t.wait_ge(sB_wd[it % 2], 16 * (it // 2 + 1))
                        for gg in range(8):
                            j = it * 8 + gg
                            if j >= 2:
                                t.wait_ge(sB_ev, j - 1)
                            for kc in range(16):
                                ins = t.matmul(
                                    qkps[j % 2][:, :],
                                    wb[it % 2][:, kc * 128:(kc + 1) * 128],
                                    ln_fm[:, kc * (NT * 128) + gg * 512:
                                          kc * (NT * 128) + (gg + 1) * 512],
                                    start=(kc == 0), stop=(kc == 15))
                                if kc == 15:
                                    ins.then_inc(sB_mm, 1)

                @blk.vector
                def _(v):
                    for j in range(256):
                        mult = SCALE if j < 128 else 1.0
                        v.wait_ge(sB_mm, j + 1)
                        if j >= 2:
                            v.wait_ge(sB_od[j % 2], 16 * (j // 2))
                        v.tensor_scalar_mul(ev[j % 2][:, :],
                                            qkps[j % 2][:, :], mult)\
                            .then_inc(sB_ev, 1)

                @blk.scalar
                def _(s):
                    for j in range(256):
                        it, gg = divmod(j, 8)
                        w, oc = divmod(it, 16)
                        dst = (q_hbm if w == 0 else k_hbm)
                        s.wait_ge(sB_ev, j + 1)
                        s.dma_start(dst[:, oc * (NT * 128) + gg * 512:
                                        oc * (NT * 128) + (gg + 1) * 512],
                                    ev[j % 2][:, :]).then_inc(sB_od[j % 2], 16)
                    s.wait_ge(sB_od[0], 16 * 128)
                    s.wait_ge(sB_od[1], 16 * 128)

        sV_od = [S("sV_od0"), S("sV_od1")]
        sV_wd = [S("sV_wd0"), S("sV_wd1")]
        sV_wc = S("sV_wc"); sV_mm = S("sV_mm")
        sV_ev = S("sV_ev"); sV_pe = S("sV_pe")

        with ExitStack() as ph:
            psb = lambda n, s, d: ph.enter_context(nc.sbuf_tensor(n, s, d))
            vwb = [psb(f"vwbB{p}", [128, 8192], BF16) for p in range(2)]
            vev = [psb(f"vevB{p}", [128, 512], BF16) for p in range(2)]
            vps = [ph.enter_context(nc.psum_tensor(f"vpsB{p}", [128, 512], F32))
                   for p in range(2)]

            with nc.Block() as blk:
                @blk.gpsimd
                def _(g):
                    for ob in range(4):
                        if ob >= 2:
                            g.wait_ge(sV_mm, NT * (ob - 1))
                        g.dma_start(
                            vwb[ob % 2][:, :].rearrange("p (kc o) -> p kc o", kc=16),
                            wvT[ob * 2048:(ob + 1) * 2048, :]
                            .rearrange("(kc p) o -> p kc o", p=128))\
                            .then_inc(sV_wd[ob % 2], 16)

                @blk.tensor
                def _(t):
                    for ob in range(4):
                        t.wait_ge(sV_wd[ob % 2], 16 * (ob // 2 + 1))
                        for i in range(NT):
                            j = ob * NT + i
                            if j >= 2:
                                t.wait_ge(sV_ev, j - 1)
                            for kc in range(16):
                                ins = t.matmul(
                                    vps[j % 2][:, :],
                                    ln_fm[:, kc * (NT * 128) + i * 128:
                                          kc * (NT * 128) + (i + 1) * 128],
                                    vwb[ob % 2][:, kc * 512:(kc + 1) * 512],
                                    start=(kc == 0), stop=(kc == 15))
                                if kc == 15:
                                    ins.then_inc(sV_mm, 1)

                @blk.vector
                def _(v):
                    for j in range(4 * NT):
                        v.wait_ge(sV_mm, j + 1)
                        if j >= 2:
                            v.wait_ge(sV_od[j % 2], 64 * (j // 2))
                        v.tensor_copy(vev[j % 2][:, :], vps[j % 2][:, :])\
                            .then_inc(sV_ev, 1)

                @blk.scalar
                def _(s):
                    for j in range(4 * NT):
                        ob, i = divmod(j, NT)
                        sq, jt = divmod(i, NTS)
                        s.wait_ge(sV_ev, j + 1)
                        for hh in range(4):
                            h = ob * 4 + hh
                            r0 = ((sq * NH + h) * NTS + jt) * 128
                            s.dma_start(v_hbm[r0:r0 + 128, :],
                                        vev[j % 2][:, hh * 128:(hh + 1) * 128])\
                                .then_inc(sV_od[j % 2], 16)
                    s.wait_ge(sV_od[0], 64 * 64)
                    s.wait_ge(sV_od[1], 64 * 64)

        phab.close()

        sC_qkv = [S("sC_qkv0"), S("sC_qkv1")]
        sC_s = S("sC_s")
        sC_sevA = S("sC_sevA"); sC_sevB = S("sC_sevB")
        sC_negmx = S("sC_negmx"); sC_exp = S("sC_exp"); sC_pnorm = S("sC_pnorm")
        sC_tg = S("sC_tg"); sC_evA = S("sC_evA")
        sC_z = S("sC_z"); sC_zev = S("sC_zev")
        sC_zd = [S("sC_zd0"), S("sC_zd1")]
        sC_hdone = S("sC_hdone")
        sP_in = [S("sP_in0"), S("sP_in1")]
        sP_out = [S("sP_out0"), S("sP_out1")]

        def cntB(r):
            hh_full, i_part = divmod(r, 16)
            return hh_full * 8 + max(0, i_part - 8)

        grp_tab = []
        row_first_g = []
        absg = 0
        for hh in range(32):
            for i in range(16):
                row = []
                row_first_g.append(absg)
                for js in range(0, i + 1, 4):
                    row.append((absg, js, min(4, i + 1 - js)))
                    absg += 1
                grp_tab.append(row)
        NGRP = absg

        with ExitStack() as ph:
            psb = lambda n, s, d: ph.enter_context(nc.sbuf_tensor(n, s, d))
            q_sb = [psb(f"qC{p}", [128, 2048], BF16) for p in range(2)]
            k_sb = [psb(f"kC{p}", [128, 2048], BF16) for p in range(2)]
            v_sb = [psb(f"vC{p}", [128, 2048], BF16) for p in range(2)]
            scores = [psb(f"scC{p}", [128, 2048], F32) for p in range(2)]
            p_sb = [psb(f"pC{p}", [128, 2048], BF16) for p in range(2)]
            pn_sb = [psb(f"pnC{p}", [128, 2048], BF16) for p in range(4)]
            pt_sb = psb("ptC", [128, 16 * 512], BF16)
            zh_sb = [psb(f"zhC{p}", [128, 2048], BF16) for p in range(2)]
            mx = [psb(f"mxC{p}", [128, 1], F32) for p in range(2)]
            mxb = [psb(f"mxbC{p}", [128, 1], F32) for p in range(2)]
            negmx = [psb(f"ngC{p}", [128, 1], F32) for p in range(2)]
            sm = [psb(f"smC{p}", [128, 1], F32) for p in range(2)]
            rc = [psb(f"rcC{p}", [128, 1], F32) for p in range(2)]
            s_ps = ph.enter_context(nc.psum_tensor("spsC", [128, 2048], F32))
            pt_ps = [ph.enter_context(nc.psum_tensor(f"ptpsC{p}", [128, 512], F32))
                     for p in range(2)]
            z_ps = [ph.enter_context(nc.psum_tensor(f"zpsC{p}", [128, 512], F32))
                    for p in range(2)]

            ptv = pt_sb[:, :].rearrange("p (j c) -> p j c", j=16)
            ptsl = pt_sb[:, :].rearrange("p (j s c) -> p j s c", j=16, s=4)

            def qkv_load(sy, hh):
                sq, h = divmod(hh, 16)
                par = hh % 2
                base = h * (NT * 128) + sq * T
                sy.dma_start(q_sb[par][:, :], q_hbm[:, base:base + T])\
                    .then_inc(sC_qkv[par], 16)
                sy.dma_start(k_sb[par][:, :], k_hbm[:, base:base + T])\
                    .then_inc(sC_qkv[par], 16)
                r0 = (sq * NH + h) * NTS * 128
                sy.dma_start(
                    v_sb[par][:, :].rearrange("p (j hd) -> p j hd", j=16),
                    v_hbm[r0:r0 + NTS * 128, :]
                    .rearrange("(j p) hd -> p j hd", p=128))\
                    .then_inc(sC_qkv[par], 16)

            with nc.Block() as blk:
                @blk.sync
                def _(sy):
                    qkv_load(sy, 0)
                    qkv_load(sy, 1)
                    for hh in range(1, 31):
                        sy.wait_ge(sC_z, 4 * hh)
                        qkv_load(sy, hh + 1)
                    sy.wait_ge(sC_zd[0], 16 * 16)
                    sy.wait_ge(sC_zd[1], 16 * 16)

                @blk.tensor
                def _(t):
                    for hh in range(32):
                        par = hh % 2
                        for k in range(19):
                            if k <= 15:
                                i = k; r = hh * 16 + i
                                if k == 0:
                                    t.wait_ge(sC_qkv[hh % 2], 48 * (hh // 2 + 1))
                                t.wait_ge(sC_sevA, r)
                                if i >= 8:
                                    t.wait_ge(sC_sevB, cntB(r))
                                tot = (i + 1) * 128
                                nch = (tot + 511) // 512
                                dch = i // 4
                                for c in range(nch):
                                    wc = min(512, tot - c * 512)
                                    t.matmul(
                                        s_ps[:, c * 512:c * 512 + wc],
                                        q_sb[par][:, i * 128:(i + 1) * 128],
                                        k_sb[par][:, c * 512:c * 512 + wc],
                                        start=True, stop=(c != dch))
                                t.matmul(s_ps[:, i * 128:(i + 1) * 128],
                                         masklT_sb[:, :], ident_sb[:, :],
                                         start=False, stop=True)\
                                    .then_inc(sC_s, 1)
                            if k >= 3:
                                i2 = k - 3; r2 = hh * 16 + i2
                                blk = i2 // 4
                                t.wait_ge(sC_pnorm, r2 + 1)
                                for (g, js, jc) in grp_tab[r2]:
                                    if g >= 2:
                                        t.wait_ge(sC_evA, g - 1)
                                    for u in range(jc):
                                        jj = js + u
                                        ins = t.matmul(
                                            pt_ps[g % 2][:, u * 128:(u + 1) * 128],
                                            pn_sb[i2 % 4][:, jj * 128:(jj + 1) * 128],
                                            ident_sb[:, :], start=True, stop=True)
                                        if u == jc - 1:
                                            ins.then_inc(sC_tg, 1)
                                if i2 % 4 == 3:
                                    zb = hh * 4 + blk
                                    if zb >= 2:
                                        t.wait_ge(sC_zev, zb - 1)
                                    g_last = row_first_g[r2] + len(grp_tab[r2]) - 1
                                    t.wait_ge(sC_evA, g_last + 1)
                                    nj = 4 * blk + 4
                                    for jj in range(nj):
                                        lo = max(0, jj - 4 * blk)
                                        ins = t.matmul(
                                            z_ps[blk % 2][:, lo * 128:512],
                                            v_sb[par][:, jj * 128:(jj + 1) * 128],
                                            ptv[:, jj, lo * 128:512],
                                            start=(jj == 0), stop=(jj == nj - 1))
                                        if jj == nj - 1:
                                            ins.then_inc(sC_z, 1)


                @blk.vector
                def _(v):
                    for hh in range(32):
                        for k in range(19):
                            if k <= 15:
                                i = k; r = hh * 16 + i
                                v.wait_ge(sC_s, r + 1)
                                if r >= 2:
                                    v.wait_ge(sC_exp, r - 1)
                                tot = (i + 1) * 128
                                wA = min(tot, 1024)
                                v.tensor_scalar(scores[i % 2][:, 0:wA],
                                                s_ps[:, 0:wA], 0.0, None,
                                                ALU.add, ALU.max,
                                                accum_out=mx[i % 2][:, 0:1])\
                                    .then_inc(sC_sevA, 1)
                                v.wait_ge(sC_sevA, r + 1)
                                if tot > 1024:
                                    v.tensor_scalar(scores[i % 2][:, 1024:tot],
                                                    s_ps[:, 1024:tot], 0.0, None,
                                                    ALU.add, ALU.max,
                                                    accum_out=mxb[i % 2][:, 0:1])\
                                        .then_inc(sC_sevB, 1)
                                    v.wait_ge(sC_sevB, cntB(r) + 1)
                                    vhop(v, v.tensor_max(mx[i % 2][:, 0:1],
                                                         mx[i % 2][:, 0:1],
                                                         mxb[i % 2][:, 0:1]))
                                v.tensor_scalar_mul(negmx[i % 2][:, 0:1],
                                                    mx[i % 2][:, 0:1], -1.0)\
                                    .then_inc(sC_negmx, 1)
                            if 1 <= k <= 16:
                                i1 = k - 1; r1 = hh * 16 + i1
                                v.wait_ge(sC_exp, r1 + 1)
                                vhop(v, v.reciprocal(rc[i1 % 2][:, 0:1],
                                                     sm[i1 % 2][:, 0:1]))
                                if r1 >= 4:
                                    g_end = row_first_g[r1 - 4] + len(grp_tab[r1 - 4])
                                    v.wait_ge(sC_tg, g_end)
                                tot = (i1 + 1) * 128
                                v.tensor_scalar(pn_sb[i1 % 4][:, 0:tot],
                                                p_sb[i1 % 2][:, 0:tot],
                                                rc[i1 % 2][:, 0:1], None, ALU.mult)\
                                    .then_inc(sC_pnorm, 1)

                @blk.scalar
                def _(s):
                    for hh in range(32):
                        for k in range(19):
                            if 1 <= k <= 16:
                                i1 = k - 1; r1 = hh * 16 + i1
                                s.wait_ge(sC_negmx, r1 + 1)
                                if r1 >= 2:
                                    s.wait_ge(sC_pnorm, r1 - 1)
                                tot = (i1 + 1) * 128
                                s.activation(p_sb[i1 % 2][:, 0:tot],
                                             scores[i1 % 2][:, 0:tot], AF.Exp,
                                             bias=negmx[i1 % 2][:, 0:1],
                                             accum_out=sm[i1 % 2][:, 0:1])\
                                    .then_inc(sC_exp, 1)
                            if k >= 3:
                                i2 = k - 3; r2 = hh * 16 + i2
                                blk = i2 // 4
                                srow = i2 % 4
                                first = True
                                for (g, js, jc) in grp_tab[r2]:
                                    s.wait_ge(sC_tg, g + 1)
                                    if first:
                                        s.wait_ge(sC_z, hh * 4 + blk)
                                        first = False
                                    s.activation(
                                        ptsl[:, js:js + jc, srow, :],
                                        pt_ps[g % 2][:, 0:jc * 128]
                                        .rearrange("p (j c) -> p j c", j=jc),
                                        AF.Copy).then_inc(sC_evA, 1)
                                if srow == 3:
                                    zb = hh * 4 + blk
                                    s.wait_ge(sC_z, zb + 1)
                                    if blk == 0 and hh >= 2:
                                        s.wait_ge(sC_zd[hh % 2], 16 * (hh // 2))
                                    s.activation(
                                        zh_sb[hh % 2][:, blk * 512:(blk + 1) * 512],
                                        z_ps[blk % 2][:, :], AF.Copy)\
                                        .then_inc(sC_zev, 1)
                        sq, h = divmod(hh, 16)
                        s.wait_ge(sC_zev, 4 * (hh + 1))
                        s.dma_start(z_hbm[:, h * (NT * 128) + sq * T:
                                          h * (NT * 128) + sq * T + T],
                                    zh_sb[hh % 2][:, :]).then_inc(sC_zd[hh % 2], 16)
                    s.wait_ge(sC_zd[0], 16 * 16)
                    s.wait_ge(sC_zd[1], 16 * 16)

                @blk.gpsimd
                def _(g):
                    np_ = 0
                    for b in range(64):
                        g.dma_start(wg_bf[b * 128:(b + 1) * 128, :]
                                    .rearrange("p (kc h) -> p kc h", kc=16),
                                    wgT[b * 2048:(b + 1) * 2048, :]
                                    .rearrange("(kc p) h -> p kc h", p=128))\
                            .then_inc(sP_out[0], 16)
                        g.dma_start(wu_bf[b * 128:(b + 1) * 128, :]
                                    .rearrange("p (kc h) -> p kc h", kc=16),
                                    wuT[b * 2048:(b + 1) * 2048, :]
                                    .rearrange("(kc p) h -> p kc h", p=128))\
                            .then_inc(sP_out[0], 16)
                        g.dma_start(wd_bf[b * 128:(b + 1) * 128, :],
                                    wdT[b * 128:(b + 1) * 128, :])\
                            .then_inc(sP_out[0], 16)
                        np_ += 3
                    g.wait_ge(sP_out[0], 16 * np_)

        sW_zd = [S("sW_zd0"), S("sW_zd1"), S("sW_zd2")]
        sW_xd = [S("sW_xd0"), S("sW_xd1")]
        sW_x2d = [S("sW_x2d0"), S("sW_x2d1")]
        sW_wd = S("sW_wd"); sW_wc = S("sW_wc")
        sW_mm = S("sW_mm"); sW_mmg = S("sW_mmg")
        sW_ev = S("sW_ev"); sW_pe = S("sW_pe")

        with ExitStack() as ph:
            psb = lambda n, s, d: ph.enter_context(nc.sbuf_tensor(n, s, d))
            c2wb = [psb(f"c2wb{p}", [128, 8192], BF16) for p in range(2)]
            zg_sb = [psb(f"zgC2{p}", [128, 8192], BF16) for p in range(3)]
            xo_sb = [psb(f"xoC2{p}", [128, 512], F32) for p in range(2)]
            x2o = [psb(f"x2oC2{p}", [128, 512], F32) for p in range(2)]
            c2ps = [ph.enter_context(nc.psum_tensor(f"c2ps{p}", [128, 512], F32))
                    for p in range(2)]

            with nc.Block() as blk:
                @blk.gpsimd
                def _(g):
                    for ob in range(4):
                        if ob >= 2:
                            g.wait_ge(sW_mm, 32 * (ob - 1))
                        g.dma_start(
                            c2wb[ob % 2][:, :].rearrange("p (dc o) -> p dc o", dc=16),
                            woT[ob * 2048:(ob + 1) * 2048, :]
                            .rearrange("(dc p) o -> p dc o", p=128))\
                            .then_inc(sW_wd, 16)

                @blk.sync
                def _(sy):
                    for ob in range(4):
                        for grp in range(NG):
                            g_it = ob * NG + grp
                            if g_it >= 3:
                                sy.wait_ge(sW_mm, 4 * (g_it - 2))
                            sy.dma_start(
                                zg_sb[g_it % 3][:, :]
                                .rearrange("p (dc c) -> p dc c", dc=16),
                                z_hbm[:, :]
                                .rearrange("p (dc t) -> p dc t", dc=16)
                                [:, :, grp * 512:(grp + 1) * 512])\
                                .then_inc(sW_zd[g_it % 3], 16)
                            for tc in range(4):
                                it = g_it * 4 + tc
                                tg = grp * 4 + tc
                                if it >= 2:
                                    sy.wait_ge(sW_ev, it - 1)
                                sy.dma_start(
                                    xo_sb[it % 2][:, :],
                                    x_in[tg * 128:(tg + 1) * 128,
                                         ob * 512:(ob + 1) * 512])\
                                    .then_inc(sW_xd[it % 2], 16)

                @blk.tensor
                def _(t):
                    for ob in range(4):
                        t.wait_ge(sW_wd, 16 * (ob + 1))
                        for grp in range(NG):
                            g_it = ob * NG + grp
                            t.wait_ge(sW_zd[g_it % 3], 16 * (g_it // 3 + 1))
                            for tc in range(4):
                                it = g_it * 4 + tc
                                if it >= 2:
                                    t.wait_ge(sW_ev, it - 1)
                                for dc in range(16):
                                    ins = t.matmul(
                                        c2ps[it % 2][:, :],
                                        zg_sb[g_it % 3][:, dc * 512 + tc * 128:
                                                        dc * 512 + (tc + 1) * 128],
                                        c2wb[ob % 2][:, dc * 512:(dc + 1) * 512],
                                        start=(dc == 0), stop=(dc == 15))
                                    if dc == 15:
                                        ins.then_inc(sW_mm, 1)

                @blk.vector
                def _(v):
                    for it in range(128):
                        v.wait_ge(sW_mm, it + 1)
                        v.wait_ge(sW_xd[it % 2], 16 * (it // 2 + 1))
                        if it >= 2:
                            v.wait_ge(sW_x2d[it % 2], 16 * (it // 2))
                        v.tensor_tensor(x2o[it % 2][:, :], c2ps[it % 2][:, :],
                                        xo_sb[it % 2][:, :], ALU.add)\
                            .then_inc(sW_ev, 1)

                @blk.scalar
                def _(s):
                    for it in range(128):
                        g_it, tc = divmod(it, 4)
                        ob, grp = divmod(g_it, NG)
                        tg = grp * 4 + tc
                        s.wait_ge(sW_ev, it + 1)
                        s.dma_start(x2_hbm[tg * 128:(tg + 1) * 128,
                                           ob * 512:(ob + 1) * 512],
                                    x2o[it % 2][:, :]).then_inc(sW_x2d[it % 2], 16)
                    s.wait_ge(sW_x2d[0], 16 * 64)
                    s.wait_ge(sW_x2d[1], 16 * 64)

        sD_x2 = [S(f"sD_x2{p}") for p in range(4)]
        sD_wgd = [S("sD_wgd0"), S("sD_wgd1")]
        sD_dwd = [S(f"sD_dwd{p}") for p in range(4)]
        sD_od = [S(f"sD_od{p}") for p in range(4)]
        sD_var = S("sD_var"); sD_std = S("sD_std")
        sD_ln = S("sD_ln"); sD_tg = S("sD_tg"); sD_tev = S("sD_tev")
        sD_gu = S("sD_gu")
        sD_g = S("sD_g"); sD_u = S("sD_u"); sD_si = S("sD_si"); sD_mu = S("sD_mu")
        sD_dcons = S("sD_dcons"); sD_d = S("sD_d")
        sD_ev = S("sD_ev"); sD_dbg = S("sD_dbg")

        sD_xo = [S(f"sD_xo{p}") for p in range(4)]

        with ExitStack() as ph:
            psb = lambda n, s, d: ph.enter_context(nc.sbuf_tensor(n, s, d))
            x2t = [psb(f"x2tD{p}", [128, D], F32) for p in range(4)]
            xo_d = [psb(f"xoD{p}", [128, 512], F32) for p in range(4)]
            ln2fm = psb("ln2fmD", [128, 16 * 512], BF16)
            gated = psb("gatedD", [128, 64 * 512], BF16)
            wgb = [psb(f"wgbD{p}", [128, 2048], BF16) for p in range(2)]
            wub = [psb(f"wubD{p}", [128, 2048], BF16) for p in range(2)]
            wdb = [psb(f"wdbD{p}", [128, 2048], BF16) for p in range(4)]
            silu_sb = [psb(f"siluD{p}", [128, 512], F32) for p in range(2)]
            outo = [psb(f"outoD{p}", [128, 512], F32) for p in range(4)]
            ln2tm = [psb(f"ln2tmD{p}", [128, D], BF16) for p in range(4)]
            bnst2 = [psb(f"bnstD{p}", [128, 24], F32) for p in range(2)]
            stat2 = [psb(f"statD{p}", [128, 2], F32) for p in range(4)]
            veps2 = [psb(f"vepsD{p}", [128, 1], F32) for p in range(4)]
            stdv2 = [psb(f"stdD{p}", [128, 1], F32) for p in range(4)]
            rstd2 = [psb(f"rstdD{p}", [128, 1], F32) for p in range(4)]
            g_ps = ph.enter_context(nc.psum_tensor("gpsD", [128, 512], F32))
            u_ps = ph.enter_context(nc.psum_tensor("upsD", [128, 512], F32))
            d_ps = [ph.enter_context(nc.psum_tensor(f"dpsD{p}", [128, 512], F32))
                    for p in range(4)]
            tr2_ps = [ph.enter_context(nc.psum_tensor(f"tr2D{p}", [128, 512], F32))
                      for p in range(2)]

            with nc.Block() as blk:
                @blk.sync
                def _(sy):
                    def x2t_loads(gx, rng=range(4)):
                        for tt in rng:
                            ti = gx * 4 + tt
                            if ti >= 4:
                                sy.wait_ge(sD_ln, ti - 3)
                            sy.dma_start(
                                x2t[ti % 4][:, :],
                                x2_hbm[ti * 128:(ti + 1) * 128, :])\
                                .then_inc(sD_x2[ti % 4], 16)

                    for g in range(NG):
                        if g == 0:
                            x2t_loads(0)
                            x2t_loads(1, range(3))
                        elif g + 1 < NG:
                            x2t_loads(g + 1, range(3))
                        def dcons_cnt(upto):
                            return upto - upto // 16

                        for ob in range(4):
                            for m in range(16):
                                st = (g * 4 + ob) * 16 + m
                                if st >= 4:
                                    if (st - 4) % 16 == 15:
                                        sy.wait_ge(sD_d, (st - 4) // 16 + 1)
                                    else:
                                        sy.wait_ge(sD_dcons, dcons_cnt(st - 3))
                                sy.dma_start(
                                    wdb[st % 4][:, :]
                                    .rearrange("p (c o) -> p c o", c=4),
                                    wd_bf[m * 512:(m + 1) * 512,
                                          ob * 512:(ob + 1) * 512]
                                    .rearrange("(c p) o -> p c o", p=128))\
                                    .then_inc(sD_dwd[st % 4], 16)
                            if ob == 0 and g + 1 < NG:
                                x2t_loads(g + 1, range(3, 4))
                            for tc in range(4):
                                oidx = (g * 4 + ob) * 4 + tc
                                tg = g * 4 + tc
                                if oidx >= 4:
                                    sy.wait_ge(sD_ev, oidx - 3)
                                sy.dma_start(
                                    xo_d[oidx % 4][:, :],
                                    x2_hbm[tg * 128:(tg + 1) * 128,
                                           ob * 512:(ob + 1) * 512])\
                                    .then_inc(sD_xo[oidx % 4], 16)

                @blk.gpsimd
                def _(g):
                    for gg in range(NG):
                        if debug and gg == 1:
                            g.wait_ge(sD_d, 4)
                            g.dma_start(dbg_ln2fm[:, :], ln2fm[:, :])\
                                .then_inc(sD_dbg, 16)
                            g.dma_start(dbg_gated[:, :], gated[:, :])\
                                .then_inc(sD_dbg, 16)
                            g.wait_ge(sD_dbg, 32)
                        for hh in range(64):
                            ghh = gg * 64 + hh
                            if ghh >= 2:
                                g.wait_ge(sD_u, ghh - 1)
                            g.dma_start(wgb[ghh % 2][:, :],
                                        wg_bf[hh * 128:(hh + 1) * 128, :])\
                                .then_inc(sD_wgd[ghh % 2], 16)
                            g.dma_start(wub[ghh % 2][:, :],
                                        wu_bf[hh * 128:(hh + 1) * 128, :])\
                                .then_inc(sD_wgd[ghh % 2], 16)

                @blk.vector
                def _(v):
                    def ln2_all(gx):
                        for tt in range(4):
                            ti = gx * 4 + tt
                            v.wait_ge(sD_x2[ti % 4], 16 * (ti // 4 + 1))
                            for q in range(4):
                                ins = v.bn_stats(
                                    bnst2[tt % 2][:, 6 * q:6 * q + 6],
                                    x2t[ti % 4][:, 512 * q:512 * (q + 1)])
                            vhop(v, ins)
                            vhop(v, v.bn_aggr(stat2[tt][:, :],
                                              bnst2[tt % 2][:, :]))
                            v.tensor_scalar_add(veps2[tt][:, 0:1],
                                                stat2[tt][:, 1:2], EPS)\
                                .then_inc(sD_var, 1)
                        for tt in range(4):
                            ti = gx * 4 + tt
                            v.wait_ge(sD_std, ti + 1)
                            vhop(v, v.reciprocal(rstd2[tt][:, 0:1],
                                                 stdv2[tt][:, 0:1]))
                            if ti >= 4:
                                v.wait_ge(sD_tg, 4 * (ti - 3))
                            v.tensor_scalar(ln2tm[tt][:, :],
                                            x2t[ti % 4][:, :],
                                            stat2[tt][:, 0:1],
                                            rstd2[tt][:, 0:1],
                                            ALU.subtract, ALU.mult)\
                                .then_inc(sD_ln, 1)

                    ln2_all(0)
                    for g in range(NG):
                        for hh in range(64):
                            ghh = g * 64 + hh
                            v.wait_ge(sD_si, ghh + 1)
                            v.wait_ge(sD_u, ghh + 1)
                            v.tensor_tensor(gated[:, hh * 512:(hh + 1) * 512],
                                            silu_sb[ghh % 2][:, :], u_ps[:, :],
                                            ALU.mult).then_inc(sD_mu, 1)
                        if g + 1 < NG:
                            ln2_all(g + 1)
                        for ob in range(4):
                            v.wait_ge(sD_d, g * 4 + ob + 1)
                            for tc in range(4):
                                oidx = (g * 4 + ob) * 4 + tc
                                if oidx >= 4:
                                    v.wait_ge(sD_od[oidx % 4], 16 * (oidx // 4))
                                v.wait_ge(sD_xo[oidx % 4], 16 * (oidx // 4 + 1))
                                v.tensor_tensor(
                                    outo[oidx % 4][:, :], d_ps[tc][:, :],
                                    xo_d[oidx % 4][:, :],
                                    ALU.add).then_inc(sD_ev, 1)

                @blk.scalar
                def _(s):
                    def sqrt4(gx):
                        for tt in range(4):
                            ti = gx * 4 + tt
                            s.wait_ge(sD_var, ti + 1)
                            if ti >= 4:
                                s.wait_ge(sD_ln, ti - 3)
                            s.activation(stdv2[tt][:, 0:1],
                                         veps2[tt][:, 0:1], AF.Sqrt)\
                                .then_inc(sD_std, 1)

                    def tevacs(gx):
                        for tt in range(4):
                            for gr in range(4):
                                gi = 4 * (gx * 4 + tt) + gr
                                s.wait_ge(sD_tg, gi + 1)
                                for c in range(4):
                                    kc = 4 * gr + c
                                    ins = s.activation(
                                        ln2fm[:, kc * 512 + tt * 128:
                                              kc * 512 + (tt + 1) * 128],
                                        tr2_ps[gi % 2][:, c * 128:(c + 1) * 128],
                                        AF.Copy)
                                    if c == 3:
                                        ins.then_inc(sD_tev, 1)

                    sqrt4(0)
                    tevacs(0)
                    for g in range(NG):
                        for hh in range(64):
                            ghh = g * 64 + hh
                            s.wait_ge(sD_g, ghh + 1)
                            if ghh >= 2:
                                s.wait_ge(sD_mu, ghh - 1)
                            s.activation(silu_sb[ghh % 2][:, :], g_ps[:, :],
                                         AF.Silu).then_inc(sD_si, 1)
                        if g + 1 < NG:
                            sqrt4(g + 1)
                        for ob in range(4):
                            for tc in range(4):
                                oidx = (g * 4 + ob) * 4 + tc
                                tg = g * 4 + tc
                                s.wait_ge(sD_ev, oidx + 1)
                                s.dma_start(out[tg * 128:(tg + 1) * 128,
                                                ob * 512:(ob + 1) * 512],
                                            outo[oidx % 4][:, :])\
                                    .then_inc(sD_od[oidx % 4], 16)
                        if g + 1 < NG:
                            tevacs(g + 1)
                    for pp in range(4):
                        s.wait_ge(sD_od[pp], 16 * 32)

                @blk.tensor
                def _(t):
                    def tgroups(gx):
                        for tt in range(4):
                            ti = gx * 4 + tt
                            t.wait_ge(sD_ln, ti + 1)
                            for gr in range(4):
                                gi = 4 * ti + gr
                                if gi >= 2:
                                    t.wait_ge(sD_tev, gi - 1)
                                for c in range(4):
                                    kc = 4 * gr + c
                                    ins = t.matmul(
                                        tr2_ps[gi % 2][:, c * 128:(c + 1) * 128],
                                        ln2tm[tt][:, kc * 128:(kc + 1) * 128],
                                        ident_sb[:, :], start=True, stop=True)
                                    if c == 3:
                                        ins.then_inc(sD_tg, 1)

                    tgroups(0)
                    for g in range(NG):
                        if debug and g == 1:
                            t.wait_ge(sD_dbg, 32)
                        t.wait_ge(sD_tev, 16 * (g + 1))
                        for hh in range(64):
                            ghh = g * 64 + hh
                            t.wait_ge(sD_wgd[ghh % 2], 32 * (ghh // 2 + 1))
                            t.wait_ge(sD_si, ghh)
                            for kc in range(16):
                                ins = t.matmul(g_ps[:, :],
                                               wgb[ghh % 2][:, kc * 128:(kc + 1) * 128],
                                               ln2fm[:, kc * 512:(kc + 1) * 512],
                                               start=(kc == 0), stop=(kc == 15))
                                if kc == 15:
                                    ins.then_inc(sD_g, 1)
                            t.wait_ge(sD_mu, ghh)
                            for kc in range(16):
                                ins = t.matmul(u_ps[:, :],
                                               wub[ghh % 2][:, kc * 128:(kc + 1) * 128],
                                               ln2fm[:, kc * 512:(kc + 1) * 512],
                                               start=(kc == 0), stop=(kc == 15))
                                if kc == 15:
                                    ins.then_inc(sD_u, 1)
                        for ob in range(4):
                            t.wait_ge(sD_ev, 4 * (g * 4 + ob))
                            for m in range(16):
                                st = (g * 4 + ob) * 16 + m
                                t.wait_ge(sD_dwd[st % 4], 16 * (st // 4 + 1))
                                t.wait_ge(sD_mu, g * 64 + 4 * m + 4)
                                wdb_st = wdb[st % 4]
                                for c in range(4):
                                    hh = 4 * m + c
                                    for tc in range(4):
                                        ins = t.matmul(
                                            d_ps[tc][:, :],
                                            gated[:, hh * 512 + tc * 128:
                                                  hh * 512 + (tc + 1) * 128],
                                            wdb_st[:, c * 512:(c + 1) * 512],
                                            start=(hh == 0), stop=(hh == 63))
                                        if hh == 63 and tc == 3:
                                            ins.then_inc(sD_d, 1)
                                        elif c == 3 and tc == 3:
                                            ins.then_inc(sD_dcons, 1)
                        if g + 1 < NG:
                            tgroups(g + 1)

    nc.finalize()
    return nc


_NC_CACHE = {}


def _get_nc(debug=False):
    key = bool(debug)
    if key not in _NC_CACHE:
        _NC_CACHE[key] = build_nc(debug=key)
    return _NC_CACHE[key]


def _prep_weights(Wq, Wk, Wv, Wo, w_gate, w_up, w_down):
    f32 = np.float32
    wq = np.ascontiguousarray(np.asarray(Wq, dtype=f32).T.reshape(2048, 16, 128)
                              .transpose(1, 0, 2)).reshape(16 * 2048, 128)
    wk = np.ascontiguousarray(np.asarray(Wk, dtype=f32).T.reshape(2048, 16, 128)
                              .transpose(1, 0, 2)).reshape(16 * 2048, 128)
    wv = np.ascontiguousarray(np.asarray(Wv, dtype=f32).T.reshape(2048, 4, 512)
                              .transpose(1, 0, 2)).reshape(4 * 2048, 512)
    wo = np.ascontiguousarray(np.asarray(Wo, dtype=f32).T.reshape(2048, 4, 512)
                              .transpose(1, 0, 2)).reshape(4 * 2048, 512)
    wg = np.ascontiguousarray(np.asarray(w_gate, dtype=f32).T.reshape(2048, 64, 128)
                              .transpose(1, 0, 2)).reshape(64 * 2048, 128)
    wu = np.ascontiguousarray(np.asarray(w_up, dtype=f32).T.reshape(2048, 64, 128)
                              .transpose(1, 0, 2)).reshape(64 * 2048, 128)
    wd = np.ascontiguousarray(np.asarray(w_down, dtype=f32).T)
    return wq, wk, wv, wo, wg, wu, wd


def kernel(L0_attn_x, L0_attn_Wq, L0_attn_Wk, L0_attn_Wv, L0_attn_Wo,
           L0_ffn_w_gate, L0_ffn_w_up, L0_ffn_w_down):
    x = np.ascontiguousarray(np.asarray(L0_attn_x, dtype=np.float32))
    wq, wk, wv, wo, wg, wu, wd = _prep_weights(
        L0_attn_Wq, L0_attn_Wk, L0_attn_Wv, L0_attn_Wo,
        L0_ffn_w_gate, L0_ffn_w_up, L0_ffn_w_down)
    masklT = np.tril(np.full((128, 128), -1e9, dtype=np.float32), k=-1)\
        .astype(ml_dtypes.bfloat16)
    ident = np.eye(128, dtype=ml_dtypes.bfloat16)

    nc = _get_nc(debug=False)
    in_maps = []
    for c in range(NCORES):
        in_maps.append({
            "x_in": x[2 * c:2 * c + 2].reshape(NSEQ * T, D),
            "wqT": wq, "wkT": wk, "wvT": wv, "woT": wo,
            "wgT": wg, "wuT": wu, "wdT": wd,
            "masklT_in": masklT, "ident_in": ident,
        })
    res = run_bass_kernel_spmd(nc, in_maps, core_ids=list(range(NCORES)))
    outs = [r["out"].reshape(NSEQ, T, D) for r in res.results]
    return np.concatenate(outs, axis=0)
